# version 06
# baseline (speedup 1.0000x reference)
import numpy as np
from contextlib import ExitStack

import concourse.bass as bass
import concourse.mybir as mybir
from concourse.bass_utils import run_bass_kernel_spmd

F32 = mybir.dt.float32
F16 = mybir.dt.float16
AF = mybir.ActivationFunctionType
ALU = mybir.AluOpType
AX = mybir.AxisListType.X

H = 4096
I = 11008
E, R = 4, 8
HT = H // 128
IT = I // 128
CW = 256
NCI = I // CW
NCO = H // CW
DHC = 512
NDH = H // DHC
ITG = 16
KZERO = I // 2
T = 4
NT = 16
NG = NT // T
TPC = NT * 128

SUB_IT = 10
FULL_IT = 5
SUBW = 2048
KSUB = float(SUBW // 2)
KFULL = float(KZERO)
KACT = float(I)
LO0, HI0 = 0.02, 0.70
WIDEN = 0.009
CCH = 2048
WACT = (2, 3)

CCHUNKS = []
_off = 0
while _off < I:
    CCHUNKS.append((_off, min(CCH, I - _off)))
    _off += min(CCH, I - _off)
NCC = len(CCHUNKS)

ITGS = []
_off = 0
while _off < IT:
    ITGS.append((_off, min(ITG, IT - _off)))
    _off += min(ITG, IT - _off)


class Plan:

    def __init__(self):
        self.ops = {e: [] for e in ("tensor", "vector", "scalar", "gpsimd", "sync")}
        self.sems = {}
        self.cnt = {}

    def add(self, eng, fn, waits=(), incs=(), tag="", dma=False):
        waits = [(s, v) for (s, v) in waits if v > 0]
        for s, v in waits:
            assert v <= self.cnt[s], f"wait on future: {s}>{self.cnt[s]} ({tag})"
        for s, k in incs:
            self.cnt[s] += k
        self.ops[eng].append((fn, waits, list(incs), tag))

    def val(self, s):
        return self.cnt[s]


def build(nc):
    p = Plan()

    x32 = nc.dram_tensor("x", [TPC, H], F32, kind="ExternalInput").ap()
    wg32 = nc.dram_tensor("wg", [128, NCI, HT, CW], F32, kind="ExternalInput").ap()
    wu32 = nc.dram_tensor("wu", [128, NCI, HT, CW], F32, kind="ExternalInput").ap()
    wb32 = nc.dram_tensor("wb", [128, NCO, HT, CW], F32, kind="ExternalInput").ap()
    wd32 = nc.dram_tensor("wd", [128, NDH, IT, DHC], F32, kind="ExternalInput").ap()
    ag32 = nc.dram_tensor("ag", [128, HT, 36], F32, kind="ExternalInput").ap()
    bp32 = nc.dram_tensor("bp", [32, H], F32, kind="ExternalInput").ap()
    br32 = nc.dram_tensor("br", [1, 36], F32, kind="ExternalInput").ap()
    id_h = nc.dram_tensor("iden", [128, 128], F16, kind="ExternalInput").ap()
    on_h = nc.dram_tensor("ones1", [1, 128], F16, kind="ExternalInput").ap()
    y32 = nc.dram_tensor("y", [TPC, H], F32, kind="ExternalOutput").ap()

    sg16 = nc.dram_tensor("sg16", [128, NCI, HT, CW], F16).ap()
    su16_ = nc.dram_tensor("su16", [128, NCI, HT, CW], F16).ap()
    sb16 = nc.dram_tensor("sb16", [128, NCO, HT, CW], F16).ap()
    sd16 = nc.dram_tensor("sd16", [128, NDH, IT, DHC], F16).ap()
    sgact = nc.dram_tensor("sgact", [NG, HT, 128, 4 * 128], F16).ap()
    thb = nc.dram_tensor("thb", [NG, 1, T * 128], F16).ap()

    ctx = ExitStack()
    e = ctx.enter_context

    s_all = e(nc.sbuf_tensor("s_all", [128, T * I], F16))
    xnT = e(nc.sbuf_tensor("xnT", [128, HT * T * 128], F16))
    wq = [e(nc.sbuf_tensor(f"wq{i_}", [128, HT * CW], F16)) for i_ in range(2)]
    xs = [e(nc.sbuf_tensor(f"xs{i_}", [128, 512], F32)) for i_ in range(2)]
    xn16 = [e(nc.sbuf_tensor(f"xn16{i_}", [128, 512], F16)) for i_ in range(2)]
    xc = [e(nc.sbuf_tensor(f"xc{i_}", [128, 512], F32)) for i_ in range(2)]
    outst = [e(nc.sbuf_tensor(f"outst{i_}", [128, 512], F32)) for i_ in range(2)]
    gst = [e(nc.sbuf_tensor(f"gst{i_}", [128, 4 * 512], F16)) for i_ in range(2)]
    pst = [e(nc.sbuf_tensor(f"pst{i_}", [128, 4 * 512], F16)) for i_ in range(2)]
    gsp = [e(nc.sbuf_tensor(f"gsp{i_}", [128, 512], F16)) for i_ in range(2)]
    ind = e(nc.sbuf_tensor("ind", [128, CCH], F16))
    bn6 = e(nc.sbuf_tensor("bn6", [128, 48], F32))
    bn2 = e(nc.sbuf_tensor("bn2", [128, 2], F32))
    mean4 = e(nc.sbuf_tensor("mean4", [128, T], F32))
    rstd4 = e(nc.sbuf_tensor("rstd4", [128, T], F32))
    blo = e(nc.sbuf_tensor("blo", [128, T], F32))
    bhi = e(nc.sbuf_tensor("bhi", [128, T], F32))
    bmid = e(nc.sbuf_tensor("bmid", [128, T], F32))
    bneg = e(nc.sbuf_tensor("bneg", [128, T], F32))
    bpr = e(nc.sbuf_tensor("bpr", [128, T], mybir.dt.int32))
    cb = e(nc.sbuf_tensor("cb", [128, 24], F32))
    acs = e(nc.sbuf_tensor("acs", [128, 24], F32))
    c4 = e(nc.sbuf_tensor("c4", [128, 2 * T], F32))
    thr4 = e(nc.sbuf_tensor("thr4", [128, T], F16))
    thrT4 = e(nc.sbuf_tensor("thrT4", [4, 128], F16))
    throw = e(nc.sbuf_tensor("throw", [1, T * 128], F16))
    TH = e(nc.sbuf_tensor("TH", [128, T * 128], F16))
    ohb = e(nc.sbuf_tensor("ohb", [128, T * E], F32))
    t4m = e(nc.sbuf_tensor("t4m", [128, T], F32))
    zm16 = e(nc.sbuf_tensor("zm16", [128, T * 32], F16))
    zTg = e(nc.sbuf_tensor("zTg", [32, T * 128], F16))
    agb = e(nc.sbuf_tensor("agb", [128, HT * 36], F16))
    bpk = e(nc.sbuf_tensor("bpk", [32, H], F16))
    brow = e(nc.sbuf_tensor("brow", [1, 36], F16))
    iden = e(nc.sbuf_tensor("idn", [128, 128], F16))
    ones = e(nc.sbuf_tensor("on1", [1, 128], F16))
    epsv = e(nc.sbuf_tensor("epsv", [128, 1], F32))
    negh = e(nc.sbuf_tensor("negh", [128, 1], F32))
    dumv = e(nc.sbuf_tensor("dumv", [128, 1], F32))
    dums = e(nc.sbuf_tensor("dums", [128, 1], F32))

    pb = [e(nc.psum_tensor(f"pb{i_}", [128, 512], F32)) for i_ in range(8)]

    names = ("q_gp", "q_sc", "s_pe", "s_ac", "s_dv",
             "wA", "wB", "xA", "xB", "cA", "cB", "gA", "gB", "pA", "pB",
             "oA", "oB", "cS", "cG", "tA", "tB")
    for name in names:
        p.sems[name] = e(nc.semaphore(name))
        p.cnt[name] = 0
    GP, SC, PE, ACT, DV = "q_gp", "q_sc", "s_pe", "s_ac", "s_dv"

    s_t = [s_all[:, t * I:(t + 1) * I] for t in range(T)]
    s3 = s_all.rearrange("p (t i) -> p t i", t=T)
    TH3 = TH.rearrange("p (t i) -> p t i", t=T)

    ms = {}

    def vflush(tail, tag):
        p.add("vector", (lambda a: lambda v: v.tensor_copy(dumv[:, 0:1], a))(tail),
              incs=[(DV, 1)], tag=tag + ":fl")

    def sflush(tail, tag, parts=128):
        p.add("scalar", (lambda a, pp: lambda s: s.copy(dums[0:pp, 0:1], a))(
            tail, parts), incs=[(ACT, 1)], tag=tag + ":fl")

    p.add("sync", lambda g: g.dma_start(iden[:, :], id_h), incs=[("cS", 16)], tag="iden")
    p.add("sync", lambda g: g.dma_start(ones[:, :], on_h), incs=[("cS", 16)], tag="ones")
    p.add("gpsimd", lambda g: g.dma_start(
        agb.rearrange("p (a b) -> p a b", a=HT), ag32), incs=[("cG", 16)], tag="agb")
    p.add("gpsimd", lambda g: g.dma_start(bpk[:, :], bp32), incs=[("cG", 16)], tag="bpk")
    p.add("gpsimd", lambda g: g.dma_start(brow[0:1, 0:36], br32), incs=[("cG", 16)], tag="brow")
    CONST_S = ("cS", 32)
    CONST_G = ("cG", 48)
    p.add("vector", lambda v: v.memset(epsv[:, :], 1e-5), incs=[(DV, 1)], tag="epsv")
    p.add("vector", lambda v: v.memset(negh[:, :], -0.5), incs=[(DV, 1)], tag="negh")
    p.add("vector", lambda v: v.tensor_copy(dumv[:, 0:1], negh[:, 0:1]),
          incs=[(DV, 1)], tag="cst:fl")
    CONST_DV = p.val(DV)

    class WStream:
        def __init__(self):
            self.n = 0
            self.scrv = {}
            self.consumed = {}

        def load(self, g0, n_free, src32, src16, scr_out, extra_waits=(), tag=""):
            i = self.n
            self.n += 1
            slot = i % 2
            wsem = "wA" if slot == 0 else "wB"
            dst = wq[slot][:, 0:n_free]

            def flat(ap):
                return ap.rearrange("p a b -> p (a b)") if len(ap.shape) == 3 else ap

            queue = "gpsimd"
            waits = list(extra_waits)
            if i >= 2:
                waits.extend(self.consumed[i - 2])
            if g0:
                p.add("gpsimd", (lambda d, s: lambda g: g.dma_start(d, s))(
                    dst, flat(src32)),
                    waits=waits, incs=[(wsem, 16)], tag=f"wl32:{tag}")
                lv = (wsem, p.val(wsem))
                p.add("scalar", (lambda d, s: lambda g: g.dma_start(d, s))(
                    flat(scr_out), dst),
                    waits=[lv], incs=[(SC, 16)], tag=f"wscr:{tag}", dma=True)
                self.scrv[i] = (SC, p.val(SC))
            else:
                p.add(queue, (lambda d, s: lambda g: g.dma_start(d, s))(
                    dst, flat(src16)),
                    waits=waits, incs=[(wsem, 16)], tag=f"wl16:{tag}")
                lv = (wsem, p.val(wsem))
            return slot, lv, i

        def done(self, i, sem_val):
            self.consumed[i] = [sem_val] + ([self.scrv[i]] if i in self.scrv else [])

    ws = WStream()

    scr_done = {}

    def xrows(tile):
        return x32[tile * 128:(tile + 1) * 128, :]

    def yrows(tile):
        return y32[tile * 128:(tile + 1) * 128, :]

    bank_free = [[] for _ in range(8)]

    def use_bank(b, waits):
        waits.extend(bank_free[b])
        bank_free[b] = []

    n_ost = 0

    def w_mid(g, t, act_flush, tag):
        p.add("vector", (lambda t_: lambda v: v.tensor_tensor(
            bmid[:, t_:t_ + 1], blo[:, t_:t_ + 1], bhi[:, t_:t_ + 1], ALU.add))(t),
            waits=[(DV, p.val(DV))], incs=[(DV, 1)], tag=f"mid0:{tag}")
        p.add("vector", (lambda t_: lambda v: v.tensor_scalar(
            bmid[:, t_:t_ + 1], bmid[:, t_:t_ + 1], 0.5, None,
            ALU.mult, ALU.bypass))(t),
            waits=[(DV, p.val(DV))], incs=[(DV, 1)], tag=f"mid1:{tag}")
        p.add("vector", (lambda t_: lambda v: v.tensor_scalar(
            bneg[:, t_:t_ + 1], bmid[:, t_:t_ + 1], -1.0, None,
            ALU.mult, ALU.bypass))(t),
            waits=[(DV, p.val(DV))], incs=[(DV, 1)], tag=f"neg:{tag}")
        if act_flush:
            p.add("vector", (lambda t_: lambda v: v.tensor_tensor(
                dumv[:, 0:1], bmid[:, t_:t_ + 1], bneg[:, t_:t_ + 1], ALU.add))(t),
                waits=[(DV, p.val(DV))], incs=[(DV, 1)], tag=f"midfl:{tag}")
        ms[f"mid:{tag}"] = p.val(DV)

    def w_upd(g, t, j, kconst, waits, last, phase):
        p.add("vector", (lambda t_: lambda v: v.tensor_tensor(
            c4[:, 2 * t_:2 * t_ + 1], c4[:, 2 * t_:2 * t_ + 1],
            c4[:, 2 * t_ + 1:2 * t_ + 2], ALU.subtract))(t),
            waits=waits, incs=[(DV, 1)], tag=f"d:{phase}:{g}:{t}:{j}")
        p.add("vector", (lambda t_, kk_: lambda v: v.tensor_scalar(
            bpr[:, t_:t_ + 1], c4[:, 2 * t_:2 * t_ + 1], kk_, None,
            ALU.is_lt, ALU.bypass))(t, kconst),
            waits=[(DV, p.val(DV))], incs=[(DV, 1)], tag=f"pred:{phase}:{g}:{t}:{j}")
        p.add("vector", (lambda t_: lambda v: v.select(
            blo[:, t_:t_ + 1], bpr[:, t_:t_ + 1], bmid[:, t_:t_ + 1],
            blo[:, t_:t_ + 1]))(t),
            waits=[(DV, p.val(DV))], incs=[(DV, 1)], tag=f"slo:{phase}:{g}:{t}:{j}")
        p.add("vector", (lambda t_: lambda v: v.select(
            bhi[:, t_:t_ + 1], bpr[:, t_:t_ + 1], bhi[:, t_:t_ + 1],
            bmid[:, t_:t_ + 1]))(t),
            waits=[(DV, p.val(DV))], incs=[(DV, 1)], tag=f"shi:{phase}:{g}:{t}:{j}")
        if phase == "F" and last:
            p.add("vector", (lambda t_: lambda v: v.tensor_tensor(
                bmid[:, t_:t_ + 1], blo[:, t_:t_ + 1], bhi[:, t_:t_ + 1], ALU.add))(t),
                waits=[(DV, p.val(DV))], incs=[(DV, 1)], tag=f"thm:{g}:{t}")
            p.add("vector", (lambda t_: lambda v: v.tensor_scalar(
                thr4[:, t_:t_ + 1], bmid[:, t_:t_ + 1], 0.5, None,
                ALU.mult, ALU.bypass))(t),
                waits=[(DV, p.val(DV))], incs=[(DV, 1)], tag=f"thr:{g}:{t}")
        elif not last:
            w_mid(g, t, phase == "F" and t in WACT, f"{phase}:{g}:{t}:{j + 1}")

    for g in range(NG):
        tiles = [g * T + t for t in range(T)]
        g0 = (g == 0)

        for t, tile in enumerate(tiles):
            for q in range(8):
                slot = q % 2
                w = []
                if q >= 2:
                    w.append((DV, ms[f"bn:{tile}:{q - 2}"]))
                elif f"p1done:{tile - 1}" in ms:
                    w.append((DV, ms[f"p1done:{tile - 1}"]))
                xsem = "xA" if slot == 0 else "xB"
                p.add("sync", (lambda sl, ap: lambda s: s.dma_start(xs[sl][:, :], ap))(
                    slot, xrows(tile)[:, q * 512:(q + 1) * 512]),
                    waits=w, incs=[(xsem, 16)], tag=f"xstat:{tile}:{q}")
                w2 = [(xsem, p.val(xsem))]
                if q == 0 and f"p1done:{tile - 1}" in ms:
                    w2.append((ACT, ms[f"rstd:{tile - 1}"]))
                p.add("vector", (lambda sl, q_: lambda v: v.bn_stats(
                    bn6[:, q_ * 6:(q_ + 1) * 6], xs[sl][:, :]))(slot, q),
                    waits=w2, incs=[(DV, 1)], tag=f"bn:{tile}:{q}")
                ms[f"bn:{tile}:{q}"] = p.val(DV)
            p.add("vector", lambda v: v.bn_aggr(bn2[:, :], bn6[:, :]),
                  waits=[(DV, ms[f"bn:{tile}:7"])], incs=[(DV, 1)], tag=f"aggr:{tile}")
            vflush(bn2[:, 1:2], f"aggr:{tile}")
            ms[f"aggr:{tile}"] = p.val(DV)
            p.add("vector", (lambda t_: lambda v: v.tensor_copy(
                mean4[:, t_:t_ + 1], bn2[:, 0:1]))(t),
                waits=[(DV, ms[f"aggr:{tile}"])], incs=[(DV, 1)], tag=f"mean:{tile}")
            ms[f"mean:{tile}"] = p.val(DV)
            p.add("scalar", (lambda t_: lambda s: s.activation(
                rstd4[:, t_:t_ + 1], bn2[:, 1:2], AF.Ln, bias=epsv[:, 0:1]))(t),
                waits=[(DV, ms[f"aggr:{tile}"]), (DV, CONST_DV)], incs=[(ACT, 1)],
                tag=f"lnv:{tile}")
            lnv = p.val(ACT)
            p.add("scalar", (lambda t_: lambda s: s.activation(
                rstd4[:, t_:t_ + 1], rstd4[:, t_:t_ + 1], AF.Exp, scale=negh[:, 0:1]))(t),
                waits=[(ACT, lnv)], incs=[(ACT, 1)], tag=f"rstd:{tile}")
            sflush(rstd4[:, t:t + 1], f"rstd:{tile}")
            ms[f"rstd:{tile}"] = p.val(ACT)

            for q in range(8):
                slot = q % 2
                w = [(DV, ms[f"nrm:{tile}:{q - 2}"])] if q >= 2 else \
                    [(DV, ms[f"bn:{tile}:{6 + slot}"])]
                xsem = "xA" if slot == 0 else "xB"
                p.add("sync", (lambda sl, ap: lambda s: s.dma_start(xs[sl][:, :], ap))(
                    slot, xrows(tile)[:, q * 512:(q + 1) * 512]),
                    waits=w, incs=[(xsem, 16)], tag=f"xn:{tile}:{q}")
                w2 = [(xsem, p.val(xsem)), (ACT, ms[f"rstd:{tile}"]), (DV, ms[f"mean:{tile}"])]
                if q >= 2:
                    w2.append((PE, ms[f"xtr:{tile}:{q - 2}"]))
                elif f"p1done:{tile - 1}" in ms:
                    w2.append((PE, ms[f"xtrl:{tile - 1}"]))
                p.add("vector", (lambda sl, t_: lambda v: v.tensor_scalar(
                    xn16[sl][:, :], xs[sl][:, :], mean4[:, t_:t_ + 1],
                    rstd4[:, t_:t_ + 1], ALU.subtract, ALU.mult))(slot, t),
                    waits=w2, incs=[(DV, 1)], tag=f"nrm:{tile}:{q}")
                vflush(xn16[slot][:, 511:512], f"nrm:{tile}:{q}")
                ms[f"nrm:{tile}:{q}"] = p.val(DV)
                bank = pb[q % 2]
                w3 = [(DV, ms[f"nrm:{tile}:{q}"]), CONST_S]
                if q >= 2:
                    w3.append((ACT, ms[f"xev:{tile}:{q - 2}"]))
                elif f"p1done:{tile - 1}" in ms:
                    w3.append((ACT, ms[f"xev:{tile - 1}:{6 + q}"]))
                use_bank(q % 2, w3)
                p.add("tensor", (lambda bk, sl: lambda tt: [tt.matmul(
                    bk[:, b_ * 128:(b_ + 1) * 128],
                    xn16[sl][:, b_ * 128:(b_ + 1) * 128],
                    iden[:, :], start=True, stop=True) for b_ in range(4)][-1])(
                    bank, slot),
                    waits=w3, incs=[(PE, 1)], tag=f"xtrb:{tile}:{q}")
                pev = p.val(PE)
                p.add("scalar", (lambda bk, q_, t_: lambda s: s.copy(
                    xnT.rearrange("p (ht x) -> p ht x", x=512)[
                        :, 4 * q_:4 * q_ + 4, t_ * 128:(t_ + 1) * 128],
                    bk[:, 0:512].rearrange("p (a b) -> p a b", a=4)))(bank, q, t),
                    waits=[(PE, pev)], incs=[(ACT, 1)], tag=f"xev:{tile}:{q}")
                if q == 7:
                    sflush(xnT[:, 31 * 512 + (t + 1) * 128 - 1:
                               31 * 512 + (t + 1) * 128], f"xev:{tile}")
                ms[f"xev:{tile}:{q}"] = p.val(ACT)
                ms[f"xtr:{tile}:{q}"] = p.val(PE)
            ms[f"xtrl:{tile}"] = p.val(PE)
            ms[f"p1done:{tile}"] = p.val(DV)
        for b_ in range(2):
            bank_free[b_] = [(ACT, p.val(ACT))]
        xnT_ready = [(ACT, ms[f"xev:{tile}:7"]) for tile in tiles]

        for c in range(NCI):
            ew = [] if g0 else [(SC, scr_done["g"])]
            slot, lv, li = ws.load(g0, HT * CW, wg32[:, c], sg16[:, c], sg16[:, c],
                                    extra_waits=ew, tag=f"g{g}c{c}")
            for t in range(T):
                k = c * T + t
                w = [lv]
                if k >= 8:
                    w.append((ACT, ms[f"sev:{g}:{k - 8}"]))
                if c == 0:
                    w.append(xnT_ready[t])
                use_bank(k % 8, w)
                p.add("tensor", (lambda bank, sl, t_: lambda tt: [
                    tt.matmul(bank[:, 0:CW],
                              xnT[:, ht * 512 + t_ * 128: ht * 512 + (t_ + 1) * 128],
                              wq[sl][:, ht * CW:(ht + 1) * CW],
                              start=(ht == 0), stop=(ht == HT - 1))
                    for ht in range(HT)][-1])(pb[k % 8], slot, t),
                    waits=w, incs=[(PE, 1)], tag=f"gmm:{g}:{k}")
                pev = p.val(PE)
                p.add("scalar", (lambda bank, t_, c_: lambda s: s.activation(
                    s_t[t_][:, c_ * CW:(c_ + 1) * CW], bank[:, 0:CW], AF.Silu))(
                    pb[k % 8], t, c),
                    waits=[(PE, pev)], incs=[(ACT, 1)], tag=f"sev:{g}:{k}")
                ms[f"sev:{g}:{k}"] = p.val(ACT)
            ws.done(li, (ACT, p.val(ACT)))
        if g0:
            scr_done["g"] = p.val(SC)
        sflush(s_t[T - 1][:, I - 1:I], f"gate:{g}")
        gate_all = (ACT, p.val(ACT))
        for b_ in range(8):
            bank_free[b_] = [gate_all]

        for t in range(T):
            p.add("vector", (lambda t_: lambda v: v.tensor_copy(
                ind[:, 0:512], s_t[t_][:, SUBW - 512:SUBW]))(t),
                waits=[(ACT, ms[f"sev:{g}:{7 * T + t}"])], incs=[(DV, 1)],
                tag=f"spad:{g}:{t}")
        for t in range(T):
            p.add("vector", (lambda t_: lambda v: v.memset(blo[:, t_:t_ + 1], LO0))(t),
                  waits=[(DV, p.val(DV))], incs=[(DV, 1)], tag=f"sblo:{g}:{t}")
            p.add("vector", (lambda t_: lambda v: v.memset(bhi[:, t_:t_ + 1], HI0))(t),
                  incs=[(DV, 1)], tag=f"sbhi:{g}:{t}")
            w_mid(g, t, False, f"S:{g}:{t}:0")
        for j in range(SUB_IT):
            for t in range(T):
                mv = ms[f"mid:S:{g}:{t}:{j}"]
                p.add("vector", (lambda t_: lambda v: v.tensor_scalar(
                    ind[:, 0:SUBW], s_t[t_][:, 0:SUBW], bmid[:, t_:t_ + 1], None,
                    ALU.is_le, ALU.add, accum_out=c4[:, 2 * t_:2 * t_ + 1]))(t),
                    waits=[(DV, mv)], incs=[(DV, 1)], tag=f"sc1:{g}:{t}:{j}")
                p.add("vector", (lambda t_: lambda v: v.tensor_scalar(
                    ind[:, 0:SUBW], s_t[t_][:, 0:SUBW], bneg[:, t_:t_ + 1], None,
                    ALU.is_lt, ALU.add, accum_out=c4[:, 2 * t_ + 1:2 * t_ + 2]))(t),
                    waits=[(DV, p.val(DV))], incs=[(DV, 1)], tag=f"sc2:{g}:{t}:{j}")
                w_upd(g, t, j, KSUB, [(DV, p.val(DV))], j == SUB_IT - 1, "S")
        for t in range(T):
            p.add("vector", (lambda t_: lambda v: v.tensor_scalar(
                blo[:, t_:t_ + 1], blo[:, t_:t_ + 1], WIDEN, 0.0,
                ALU.subtract, ALU.max))(t),
                waits=[(DV, p.val(DV))], incs=[(DV, 1)], tag=f"wlo:{g}:{t}")
            p.add("vector", (lambda t_: lambda v: v.tensor_scalar(
                bhi[:, t_:t_ + 1], bhi[:, t_:t_ + 1], WIDEN, None,
                ALU.add, ALU.bypass))(t),
                waits=[(DV, p.val(DV))], incs=[(DV, 1)], tag=f"whi:{g}:{t}")
            w_mid(g, t, t in WACT, f"F:{g}:{t}:0")
        for t in range(T):
            p.add("vector", (lambda t_: lambda v: v.tensor_copy(
                ind[:, 0:512], s_t[t_][:, I - 512:I]))(t),
                waits=[gate_all], incs=[(DV, 1)], tag=f"fpad:{g}:{t}")
        pads_done = p.val(DV)

        def mk_dve_unit(t, j):
            def emit():
                mv = ms[f"mid:F:{g}:{t}:{j}"]
                for q, (off, w_) in enumerate(CCHUNKS):
                    p.add("vector", (lambda t_, o_, w__, q_: lambda v: v.tensor_scalar(
                        ind[:, 0:w__], s_t[t_][:, o_:o_ + w__], bmid[:, t_:t_ + 1],
                        None, ALU.is_le, ALU.add,
                        accum_out=cb[:, t_ * 12 + q_:t_ * 12 + q_ + 1]))(t, off, w_, q),
                        waits=[(DV, mv), (DV, pads_done)], incs=[(DV, 1)],
                        tag=f"fc1:{g}:{t}:{j}:{q}")
                for q, (off, w_) in enumerate(CCHUNKS):
                    p.add("vector", (lambda t_, o_, w__, q_: lambda v: v.tensor_scalar(
                        ind[:, 0:w__], s_t[t_][:, o_:o_ + w__], bneg[:, t_:t_ + 1],
                        None, ALU.is_lt, ALU.add,
                        accum_out=cb[:, t_ * 12 + 6 + q_:t_ * 12 + 7 + q_]))(t, off, w_, q),
                        waits=[(DV, p.val(DV))], incs=[(DV, 1)],
                        tag=f"fc2:{g}:{t}:{j}:{q}")
                allc = p.val(DV)
                p.add("vector", (lambda t_: lambda v: v.reduce_sum(
                    c4[:, 2 * t_:2 * t_ + 1], cb[:, t_ * 12:t_ * 12 + 6], AX))(t),
                    waits=[(DV, allc)], incs=[(DV, 1)], tag=f"fr1:{g}:{t}:{j}")
                p.add("vector", (lambda t_: lambda v: v.reduce_sum(
                    c4[:, 2 * t_ + 1:2 * t_ + 2], cb[:, t_ * 12 + 6:t_ * 12 + 12],
                    AX))(t),
                    waits=[(DV, p.val(DV))], incs=[(DV, 1)], tag=f"fr2:{g}:{t}:{j}")
                ms[f"fcnt:{g}:{t}:{j}"] = p.val(DV)
                w_upd(g, t, j, KFULL, [(DV, p.val(DV))], j == FULL_IT - 1, "F")
            return emit

        def mk_act_chunks(t, j, qr):
            def emit():
                mv = ms[f"mid:F:{g}:{t}:{j}"]
                a0 = (t - 2) * 12
                for qq in qr:
                    side, q = qq // 6, qq % 6
                    off, w_ = CCHUNKS[q]
                    bias = bneg if side == 0 else bmid
                    p.add("scalar", (lambda t_, o_, w__, b_, sl_: lambda s: s.activation(
                        pst[0][:, 0:w__], s_t[t_][:, o_:o_ + w__], AF.Sign,
                        bias=b_[:, t_:t_ + 1],
                        accum_out=acs[:, sl_:sl_ + 1]))(t, off, w_, bias, a0 + qq),
                        waits=[(DV, mv)], incs=[(ACT, 1)], tag=f"ac:{g}:{t}:{j}:{qq}")
            return emit

        def mk_act_tail(t, j):
            def emit():
                a0 = (t - 2) * 12
                sflush(acs[:, a0 + 11:a0 + 12], f"acfl:{g}:{t}:{j}")
                fv = p.val(ACT)
                ms[f"acnt:{g}:{t}:{j}"] = fv
                p.add("vector", (lambda t_, a_: lambda v: v.reduce_sum(
                    c4[:, 2 * t_:2 * t_ + 1], acs[:, a_ + 6:a_ + 12], AX))(t, a0),
                    waits=[(ACT, fv), (DV, p.val(DV))], incs=[(DV, 1)],
                    tag=f"ar2:{g}:{t}:{j}")
                p.add("vector", (lambda t_, a_: lambda v: v.reduce_sum(
                    c4[:, 2 * t_ + 1:2 * t_ + 2], acs[:, a_:a_ + 6], AX))(t, a0),
                    waits=[(DV, p.val(DV))], incs=[(DV, 1)], tag=f"ar1:{g}:{t}:{j}")
                w_upd(g, t, j, KACT, [(DV, p.val(DV))], j == FULL_IT - 1, "F")
            return emit

        master = []
        for j in range(FULL_IT):
            for td, ta in ((0, 2), (1, 3)):
                master.append(mk_dve_unit(td, j))
                master.append(mk_act_chunks(ta, j, range(0, 4)))
                master.append(mk_act_chunks(ta, j, range(4, 8)))
                master.append(mk_act_chunks(ta, j, range(8, 12)))
                master.append(mk_act_tail(ta, j))
        m_emitted = 0

        for t, tile in enumerate(tiles):
            bank = pb[t]
            w = [gate_all, CONST_G, CONST_S]
            use_bank(t, w)
            p.add("tensor", (lambda bk, t_: lambda tt: (
                [tt.matmul(bk[:, 0:36],
                           xnT[:, ht * 512 + t_ * 128: ht * 512 + (t_ + 1) * 128],
                           agb[:, ht * 36:(ht + 1) * 36],
                           start=(ht == 0), stop=False) for ht in range(HT)],
                tt.matmul(bk[:, 0:36], ones[0:1, :], brow[0:1, 0:36],
                          start=False, stop=True))[1])(bank, t),
                waits=w, incs=[(PE, 1)], tag=f"zl:{tile}")
            pev = p.val(PE)
            p.add("vector", (lambda bk, t_: lambda v: v.reduce_max(
                t4m[:, t_:t_ + 1], bk[:, 32:36], AX))(bank, t),
                waits=[(PE, pev)], incs=[(DV, 1)], tag=f"rmax:{tile}")
            rv = p.val(DV)
            p.add("vector", (lambda bk, t_: lambda v: v.tensor_scalar(
                ohb[:, t_ * E:(t_ + 1) * E], bk[:, 32:36], t4m[:, t_:t_ + 1], None,
                ALU.is_ge, ALU.bypass))(bank, t),
                waits=[(DV, rv)], incs=[(DV, 1)], tag=f"oh:{tile}")
            ov = p.val(DV)
            for ee in range(E):
                p.add("vector", (lambda bk, t_, e_: lambda v: v.tensor_scalar(
                    zm16[:, t_ * 32 + e_ * R: t_ * 32 + (e_ + 1) * R],
                    bk[:, e_ * R:(e_ + 1) * R],
                    ohb[:, t_ * E + e_: t_ * E + e_ + 1], None,
                    ALU.mult, ALU.bypass))(bank, t, ee),
                    waits=[(DV, ov)], incs=[(DV, 1)], tag=f"zm:{tile}:{ee}")
            vflush(zm16[:, t * 32 + 31: t * 32 + 32], f"zm:{tile}")
            zmv = p.val(DV)
            bank2 = pb[4 + (t % 2)]
            w = [(DV, zmv)]
            if t >= 2:
                w.append((ACT, ms[f"zev:{tiles[t - 2]}"]))
            use_bank(4 + (t % 2), w)
            p.add("tensor", (lambda b2, t_: lambda tt: tt.matmul(
                b2[0:32, 0:128], zm16[:, t_ * 32:(t_ + 1) * 32], iden[:, :],
                start=True, stop=True))(bank2, t),
                waits=w, incs=[(PE, 1)], tag=f"ztr:{tile}")
            pez = p.val(PE)
            p.add("scalar", (lambda b2, t_: lambda s: s.copy(
                zTg[0:32, t_ * 128:(t_ + 1) * 128], b2[0:32, 0:128]))(bank2, t),
                waits=[(PE, pez)], incs=[(ACT, 1)], tag=f"zev:{tile}")
            if t == T - 1:
                sflush(zTg[0:32, T * 128 - 1:T * 128], f"zev:{tile}", parts=32)
            ms[f"zev:{tile}"] = p.val(ACT)
        zT_ready = (ACT, p.val(ACT))
        zread_done = (DV, p.val(DV))
        for b_ in range(4):
            bank_free[b_] = [zread_done]
        for b_ in (4, 5):
            bank_free[b_] = [zT_ready]

        for c in range(NCO):
            ew = [] if g0 else [(SC, scr_done["b"])]
            slot, lv, li = ws.load(g0, HT * CW, wb32[:, c], sb16[:, c], sb16[:, c],
                                    extra_waits=ew, tag=f"b{g}c{c}")
            for ob in range(2):
                og = c * 2 + ob
                bk = pb[4 + og % 2]
                w = [lv, zT_ready]
                if og >= 2:
                    w.append((ACT, ms[f"gev:{g}:{og - 2}"]))
                use_bank(4 + og % 2, w)
                p.add("tensor", (lambda bk_, sl, ob_, og_: lambda tt: (
                    [tt.matmul(bk_[:, :],
                               wq[sl][:, ht * CW + ob_ * 128: ht * CW + (ob_ + 1) * 128],
                               xnT[:, ht * 512:(ht + 1) * 512],
                               start=(ht == 0), stop=False) for ht in range(HT)],
                    tt.matmul(bk_[:, :], bpk[0:32, og_ * 128:(og_ + 1) * 128],
                              zTg[0:32, :], start=False, stop=True))[1])(bk, slot, ob, og),
                    waits=w, incs=[(PE, 1)], tag=f"bmm:{g}:{og}")
                pev = p.val(PE)
                gslot = og % 2
                w2 = [(PE, pev)]
                if og >= 2:
                    w2.append(ms[f"gsp:{g}:{og - 2}"])
                elif g > 0:
                    w2.append(ms[f"gsp:{g - 1}:{2 * NCO - 2 + og}"])
                p.add("scalar", (lambda bk_, gs: lambda s: s.activation(
                    gsp[gs][:, :], bk_[:, :], AF.Silu))(bk, gslot),
                    waits=w2, incs=[(ACT, 1)], tag=f"gev:{g}:{og}")
                sflush(gsp[gslot][:, 511:512], f"gev:{g}:{og}")
                ms[f"gev:{g}:{og}"] = p.val(ACT)
                av = p.val(ACT)
                psem = "pA" if gslot == 0 else "pB"
                p.add("scalar", (lambda gs, g_, og_: lambda s: s.dma_start(
                    sgact[g_, og_], gsp[gs][:, :]))(gslot, g, og),
                    waits=[(ACT, av)], incs=[(psem, 16)], tag=f"gsp:{g}:{og}", dma=True)
                ms[f"gsp:{g}:{og}"] = (psem, p.val(psem))
                tgt = (og + 1) * len(master) // (2 * NCO)
                while m_emitted < tgt:
                    master[m_emitted]()
                    m_emitted += 1
            ws.done(li, (ACT, p.val(ACT)))
        if g0:
            scr_done["b"] = p.val(SC)
        while m_emitted < len(master):
            master[m_emitted]()
            m_emitted += 1
        gact_pA = ("pA", p.val("pA"))
        gact_pB = ("pB", p.val("pB"))
        bank_free[4] = [(ACT, ms[f"gev:{g}:{2 * NCO - 2}"])]
        bank_free[5] = [(ACT, ms[f"gev:{g}:{2 * NCO - 1}"])]
        cnt_done_dv = (DV, p.val(DV))

        p.add("vector", lambda v: v.reduce_sum(dumv[:, 0:1], thr4[:, 0:T], AX),
              waits=[(DV, p.val(DV))], incs=[(DV, 1)], tag=f"th4fl:{g}")
        th4fl = p.val(DV)
        w = [(DV, th4fl), CONST_S]
        use_bank(6, w)
        p.add("tensor", lambda tt: tt.matmul(
            pb[6][0:4, 0:128], thr4[:, 0:T], iden[:, :], start=True, stop=True),
            waits=w, incs=[(PE, 1)], tag=f"thtr:{g}")
        pev = p.val(PE)
        p.add("scalar", lambda s: s.copy(thrT4[0:4, :], pb[6][0:4, 0:128]),
              waits=[(PE, pev)], incs=[(ACT, 1)], tag=f"thev:{g}")
        sflush(thrT4[0:4, 127:128], f"thev:{g}", parts=4)
        thev = p.val(ACT)
        w = [(ACT, thev)]
        if g > 0:
            w.append(("tB", 16 * g))
        p.add("sync", (lambda g_: lambda s: s.dma_start(
            thb[g_, 0].rearrange("(a b) -> a b", a=4), thrT4[0:4, :]))(g),
            waits=w, incs=[("tA", 16)], tag=f"thdo:{g}")
        w = [("tA", p.val("tA"))]
        if g > 0:
            w.append((PE, ms[f"thbc:{g - 1}"]))
        p.add("sync", (lambda g_: lambda s: s.dma_start(throw[0:1, :], thb[g_]))(g),
            waits=w, incs=[("tB", 16)], tag=f"thdi:{g}")
        w = [("tB", p.val("tB")), CONST_S]
        use_bank(6, w)
        p.add("tensor", lambda tt: tt.matmul(
            pb[6][:, 0:T * 128], ones[0:1, :], throw[0:1, :], start=True, stop=True),
            waits=w, incs=[(PE, 1)], tag=f"thbc:{g}")
        ms[f"thbc:{g}"] = p.val(PE)
        w = [(PE, ms[f"thbc:{g}"])]
        if g > 0:
            w.append((DV, ms[f"m:{g - 1}:{2 * NCI - 1}"]))
        p.add("scalar", lambda s: s.copy(TH[:, :], pb[6][:, 0:T * 128]),
              waits=w, incs=[(ACT, 1)], tag=f"THev:{g}")
        sflush(TH[:, T * 128 - 1:T * 128], f"THev:{g}")
        th_ready = p.val(ACT)
        bank_free[6] = [(ACT, th_ready)]

        for c in range(NCI):
            ew = [] if g0 else [(SC, scr_done["u"])]
            slot, lv, li = ws.load(g0, HT * CW, wu32[:, c], su16_[:, c], su16_[:, c],
                                    extra_waits=ew, tag=f"u{g}c{c}")
            for bb in range(2):
                it = c * 2 + bb
                ub = pb[it % 3]
                tb = pb[3 + it % 2]
                w = [gate_all]
                if it >= 2:
                    w.append((ACT, ms[f"smev:{g}:{it - 2}"]))
                use_bank(3 + it % 2, w)
                p.add("tensor", (lambda it_, tb_: lambda tt: [
                    tt.matmul(tb_[:, t_ * 128:(t_ + 1) * 128],
                              s3[:, t_, it_ * 128:(it_ + 1) * 128], iden[:, :],
                              start=True, stop=True) for t_ in range(T)][-1])(it, tb),
                    waits=w, incs=[(PE, 1)], tag=f"smtr:{g}:{it}")
                ptr = p.val(PE)
                p.add("scalar", (lambda tb_, it_: lambda s: s.copy(
                    s3[:, :, it_ * 128:(it_ + 1) * 128],
                    tb_[:, :].rearrange("p (t i) -> p t i", t=T)))(tb, it),
                    waits=[(PE, ptr), cnt_done_dv], incs=[(ACT, 1)],
                    tag=f"smev:{g}:{it}")
                ms[f"smev:{g}:{it}"] = p.val(ACT)
                ar = (it % 2) * 512
                mr = 1024 + (it % 2) * 512
                w2 = [(ACT, ms[f"smev:{g}:{it}"])]
                if it >= 2:
                    w2.append((DV, ms[f"m:{g}:{it - 2}"]))
                elif g > 0:
                    w2.append((DV, ms[f"m:{g - 1}:{2 * NCI - 2 + it}"]))
                p.add("scalar", (lambda it_, ar_: lambda s: s.activation(
                    ind[:, ar_:ar_ + 512].rearrange("p (t i) -> p t i", t=T),
                    s3[:, :, it_ * 128:(it_ + 1) * 128], AF.Abs))(it, ar),
                    waits=w2, incs=[(ACT, 1)], tag=f"sab:{g}:{it}")
                sflush(ind[:, ar + 511:ar + 512], f"sab:{g}:{it}")
                abfl = p.val(ACT)
                p.add("vector", (lambda ar_, mr_: lambda v: v.tensor_tensor(
                    ind[:, mr_:mr_ + 512].rearrange("p (t i) -> p t i", t=T),
                    ind[:, ar_:ar_ + 512].rearrange("p (t i) -> p t i", t=T),
                    TH3[:, :, :],
                    ALU.is_gt))(ar, mr),
                    waits=[(ACT, abfl), (ACT, th_ready), (DV, p.val(DV))],
                    incs=[(DV, 1)], tag=f"m:{g}:{it}")
                ms[f"m:{g}:{it}"] = p.val(DV)
                p.add("vector", (lambda it_, mr_: lambda v: v.tensor_tensor(
                    s3[:, :, it_ * 128:(it_ + 1) * 128],
                    ind[:, mr_:mr_ + 512].rearrange("p (t i) -> p t i", t=T),
                    s3[:, :, it_ * 128:(it_ + 1) * 128], ALU.mult))(it, mr),
                    waits=[(DV, p.val(DV))], incs=[(DV, 1)], tag=f"sm:{g}:{it}")
                smv = p.val(DV)
                w = [lv]
                if it >= 3:
                    w.append((DV, ms[f"prod:{g}:{it - 3}"]))
                use_bank(it % 3, w)
                p.add("tensor", (lambda sl, bb_, ub_: lambda tt: [
                    tt.matmul(ub_[:, :],
                              wq[sl][:, ht * CW + bb_ * 128: ht * CW + (bb_ + 1) * 128],
                              xnT[:, ht * 512:(ht + 1) * 512],
                              start=(ht == 0), stop=(ht == HT - 1))
                    for ht in range(HT)][-1])(slot, bb, ub),
                    waits=w, incs=[(PE, 1)], tag=f"ump:{g}:{it}")
                pup = p.val(PE)
                p.add("vector", (lambda it_, ub_: lambda v: v.tensor_tensor(
                    s3[:, :, it_ * 128:(it_ + 1) * 128],
                    s3[:, :, it_ * 128:(it_ + 1) * 128],
                    ub_[:, :].rearrange("p (t i) -> p t i", t=T), ALU.mult))(it, ub),
                    waits=[(PE, pup), (DV, smv)], incs=[(DV, 1)],
                    tag=f"prod:{g}:{it}")
                vflush(s_all[:, (T - 1) * I + it * 128 + 127:
                             (T - 1) * I + it * 128 + 128], f"prod:{g}:{it}")
                ms[f"prod:{g}:{it}"] = p.val(DV)
            ws.done(li, (PE, p.val(PE)))
        if g0:
            scr_done["u"] = p.val(SC)
        prod_all = (DV, p.val(DV))
        for b_ in range(8):
            bank_free[b_] = [prod_all]

        pend_epi = []
        for hc in range(NDH):
            bset_i = [4 * (hc % 2) + i_ for i_ in range(4)]
            bset = [pb[i_] for i_ in bset_i]
            gslot = hc % 2
            w = [gact_pA, gact_pB]
            if hc >= 2:
                w.append((DV, ms[f"pprod:{g}:{hc - 2}"]))
            elif g > 0:
                w.append((DV, ms[f"pprod:{g - 1}:{NDH - 2 + hc}"]))
            gsem = "gA" if gslot == 0 else "gB"
            p.add("scalar", (lambda g_, hc_, gs: lambda s: s.dma_start(
                gst[gs][:, :].rearrange("p (a b) -> p a b", a=4),
                sgact[g_].rearrange("a p b -> p a b")[:, 4 * hc_:4 * hc_ + 4, :]))(
                g, hc, gslot),
                waits=w, incs=[(gsem, 16)], tag=f"gfill:{g}:{hc}", dma=True)
            gfv = (gsem, p.val(gsem))
            for gi, (it0, itn) in enumerate(ITGS):
                ew = [] if g0 else [(SC, scr_done["d"])]
                slot, lv, li = ws.load(g0, itn * DHC, wd32[:, hc, it0:it0 + itn, :],
                                        sd16[:, hc, it0:it0 + itn, :],
                                        sd16[:, hc, it0:it0 + itn, :],
                                        extra_waits=ew, tag=f"d{g}h{hc}i{gi}")
                w = [lv, prod_all]
                if gi == 0:
                    for b_ in bset_i:
                        use_bank(b_, w)
                    if hc >= 2:
                        w.append((ACT, ms[f"sav:{g}:{hc - 2}"]))
                    if hc >= 3:
                        w.append((DV, ms[f"cmb:{g}:{hc - 3}:{T - 1}"]))
                p.add("tensor", (lambda sl, it0_, itn_, bs, first: lambda tt: [
                    tt.matmul(bs[hb][:, :],
                              wq[sl][:, il * DHC + hb * 128: il * DHC + (hb + 1) * 128],
                              s3[:, :, (it0_ + il) * 128:(it0_ + il + 1) * 128],
                              start=(first and il == 0), stop=((it0_ + il) == IT - 1))
                    for il in range(itn_) for hb in range(4)][-1])(
                    slot, it0, itn, bset, gi == 0),
                    waits=w, incs=[(PE, 1)], tag=f"dmm:{g}:{hc}:{gi}")
                ws.done(li, (PE, p.val(PE)))
            pdn = p.val(PE)
            if len(pend_epi) == 2:
                pend_epi.pop(0)()
            pslot = hc % 2
            for hb in range(4):
                w = [(PE, p.val(PE) if hb == 0 else pdn)]
                if hb == 0 and hc >= 2:
                    w.append((DV, ms[f"pprod:{g}:{hc - 2}"]))
                elif hb == 0 and g > 0:
                    w.append((DV, ms[f"pprod:{g - 1}:{NDH - 2 + hc}"]))
                p.add("scalar", (lambda b_, ps, hb_: lambda s: s.copy(
                    pst[ps][:, hb_ * 512:(hb_ + 1) * 512], b_[:, :]))(
                    bset[hb], pslot, hb),
                    waits=w, incs=[(ACT, 1)], tag=f"sav:{g}:{hc}:{hb}")
            sev = p.val(ACT)
            ms[f"sav:{g}:{hc}"] = sev
            p.add("vector", (lambda ps: lambda v: v.tensor_tensor(
                pst[ps][:, :], pst[ps][:, :], gst[ps][:, :], ALU.mult))(pslot),
                waits=[(ACT, sev), gfv], incs=[(DV, 1)], tag=f"pprod:{g}:{hc}")
            vflush(pst[pslot][:, 2047:2048], f"pprod:{g}:{hc}")
            ms[f"pprod:{g}:{hc}"] = p.val(DV)
            ppv = p.val(DV)

            def mk_epi(hc, pslot, ppv):
                def emit():
                    nonlocal n_ost
                    for t, tile in enumerate(tiles):
                        tb = pb[4 * ((hc + 1) % 2) + t]
                        w0 = [(DV, ppv)]
                        if t == 0:
                            if f"sav:{g}:{hc + 1}" in ms:
                                w0.append((ACT, ms[f"sav:{g}:{hc + 1}"]))
                            if f"cmb:{g}:{hc - 2}:{T - 1}" in ms:
                                w0.append((DV, ms[f"cmb:{g}:{hc - 2}:{T - 1}"]))
                        p.add("tensor", (lambda ps, t_, tb_: lambda tt: [
                            tt.matmul(tb_[:, hb * 128:(hb + 1) * 128],
                                      pst[ps][:, hb * 512 + t_ * 128:
                                              hb * 512 + (t_ + 1) * 128],
                                      iden[:, :], start=True, stop=True)
                            for hb in range(4)][-1])(pslot, t, tb),
                            waits=w0, incs=[(PE, 1)], tag=f"ptr:{g}:{hc}:{t}")
                        ptv = p.val(PE)
                        idx = hc * T + t
                        xslot = idx % 2
                        w = []
                        if idx >= 2:
                            pi = idx - 2
                            w.append((DV, ms[f"cmb:{g}:{pi // T}:{pi % T}"]))
                        elif g > 0:
                            w.append((DV, ms[f"cmb:{g - 1}:{NDH - 1}:{(idx - 2) % T}"]))
                        csem = "cA" if xslot == 0 else "cB"
                        p.add("sync", (lambda xs_, tile_, hc_: lambda s: s.dma_start(
                            xc[xs_][:, :], xrows(tile_)[:, hc_ * 512:(hc_ + 1) * 512]))(
                            xslot, tile, hc),
                            waits=w, incs=[(csem, 16)], tag=f"xc:{g}:{hc}:{t}")
                        xlv = (csem, p.val(csem))
                        oslot = idx % 2
                        w = [(PE, ptv), xlv]
                        if n_ost >= 2:
                            osm = "oA" if n_ost % 2 == 0 else "oB"
                            w.append((osm, 16 * (n_ost // 2)))
                        p.add("vector", (lambda xs_, tb_, os_: lambda v: v.tensor_tensor(
                            outst[os_][:, :], xc[xs_][:, :], tb_[:, :], ALU.add))(
                            xslot, tb, oslot),
                            waits=w, incs=[(DV, 1)], tag=f"cmb:{g}:{hc}:{t}")
                        ms[f"cmb:{g}:{hc}:{t}"] = p.val(DV)
                        cv = p.val(DV)
                        osem = "oA" if oslot == 0 else "oB"
                        p.add("sync", (lambda os_, tile_, hc_: lambda s: s.dma_start(
                            yrows(tile_)[:, hc_ * 512:(hc_ + 1) * 512], outst[os_][:, :]))(
                            oslot, tile, hc),
                            waits=[(DV, cv)], incs=[(osem, 16)], tag=f"ost:{g}:{hc}:{t}")
                        n_ost += 1
                return emit

            pend_epi.append(mk_epi(hc, pslot, ppv))
        for f_ in pend_epi:
            f_()
        pend_epi = []
        if g0:
            scr_done["d"] = p.val(SC)
        last_cmb = (DV, p.val(DV))
        for b_ in range(8):
            bank_free[b_] = [last_cmb]

    p.add("sync", lambda s: s.nop(),
          waits=[("oA", 16 * ((n_ost + 1) // 2)), ("oB", 16 * (n_ost // 2))],
          tag="final")

    with nc.Block() as block:
        def runner(name):
            def _run(eng):
                for fn, waits, incs, tag in p.ops[name]:
                    for s, v in waits:
                        eng.wait_ge(p.sems[s], v)
                    inst = fn(eng)
                    for s, k in incs:
                        inst.then_inc(p.sems[s], k)
            return _run
        block.tensor(runner("tensor"))
        block.vector(runner("vector"))
        block.scalar(runner("scalar"))
        block.gpsimd(runner("gpsimd"))
        block.sync(runner("sync"))

    ctx.close()
    return nc


def _blocked_hi(wT, nch):
    return np.ascontiguousarray(wT.reshape(HT, 128, nch, CW).transpose(1, 2, 0, 3))


def _blocked_down(wdT):
    return np.ascontiguousarray(wdT.reshape(IT, 128, NDH, DHC).transpose(1, 2, 0, 3))


def make_inputs(L0_attn_x, L0_attn_w_gate, L0_attn_w_up, L0_attn_w_down,
                L0_ffn_W_up, L0_ffn_gate_w, L0_ffn_expert_bias,
                L0_ffn_lora_A, L0_ffn_lora_B):
    x2 = np.ascontiguousarray(
        np.asarray(L0_attn_x, dtype=np.float32).reshape(8 * TPC, H))
    wg = _blocked_hi(np.ascontiguousarray(np.asarray(L0_attn_w_gate, np.float32).T), NCI)
    wu = _blocked_hi(np.ascontiguousarray(np.asarray(L0_attn_w_up, np.float32).T), NCI)
    wb = _blocked_hi(np.ascontiguousarray(np.asarray(L0_ffn_W_up, np.float32).T), NCO)
    wd = _blocked_down(np.ascontiguousarray(np.asarray(L0_attn_w_down, np.float32).T))
    A_all = np.asarray(L0_ffn_lora_A, np.float32).reshape(32, H)
    ag = np.concatenate([A_all.T,
                         np.ascontiguousarray(np.asarray(L0_ffn_gate_w, np.float32).T)],
                        axis=1)
    ag = np.ascontiguousarray(ag.reshape(HT, 128, 36).transpose(1, 0, 2))
    bp = np.ascontiguousarray(
        np.asarray(L0_ffn_lora_B, np.float32).transpose(0, 2, 1).reshape(32, H))
    br = np.concatenate([np.zeros((1, 32), np.float32),
                         np.asarray(L0_ffn_expert_bias, np.float32).reshape(1, 4)],
                        axis=1)
    shared = dict(wg=wg, wu=wu, wb=wb, wd=wd, ag=ag, bp=bp, br=br,
                  iden=np.eye(128, dtype=np.float16),
                  ones1=np.ones((1, 128), dtype=np.float16))
    return x2, shared


def kernel(**inputs):
    x2, shared = make_inputs(**inputs)
    nc = bass.Bass("TRN2", target_bir_lowering=False)
    build(nc)
    in_maps = []
    for c in range(8):
        m = dict(shared)
        m["x"] = np.ascontiguousarray(x2[c * TPC:(c + 1) * TPC])
        in_maps.append(m)
    res = run_bass_kernel_spmd(nc, in_maps, core_ids=list(range(8)))
    out = np.concatenate([r["y"] for r in res.results], axis=0)
    return out.reshape(4, 4096, 4096)


# version 07
# speedup vs baseline: 1.0868x; 1.0868x over previous
import numpy as np
from contextlib import ExitStack

import concourse.bass as bass
import concourse.mybir as mybir
from concourse.bass_utils import run_bass_kernel_spmd

F32 = mybir.dt.float32
F16 = mybir.dt.float16
AF = mybir.ActivationFunctionType
ALU = mybir.AluOpType
AX = mybir.AxisListType.X

H = 4096
I = 11008
E, R = 4, 8
HT = H // 128
IT = I // 128
CW = 256
NCI = I // CW
NCO = H // CW
DHC = 512
NDH = H // DHC
ITG = 16
KZERO = I // 2
T = 4
NT = 16
NG = NT // T
TPC = NT * 128

SUB_IT = 10
FULL_IT = 5
SUBW = 2048
KSUB = float(SUBW // 2)
KFULL = float(KZERO)
KACT = float(I)
LO0, HI0 = 0.02, 0.70
WIDEN = 0.009
CCH = 2048
WACT = (2, 3)

CCHUNKS = []
_off = 0
while _off < I:
    CCHUNKS.append((_off, min(CCH, I - _off)))
    _off += min(CCH, I - _off)
NCC = len(CCHUNKS)

ITGS = []
_off = 0
while _off < IT:
    ITGS.append((_off, min(ITG, IT - _off)))
    _off += min(ITG, IT - _off)


class Plan:

    def __init__(self):
        self.ops = {e: [] for e in ("tensor", "vector", "scalar", "gpsimd", "sync")}
        self.sems = {}
        self.cnt = {}

    def add(self, eng, fn, waits=(), incs=(), tag="", dma=False):
        waits = [(s, v) for (s, v) in waits if v > 0]
        for s, v in waits:
            assert v <= self.cnt[s], f"wait on future: {s}>{self.cnt[s]} ({tag})"
        for s, k in incs:
            self.cnt[s] += k
        self.ops[eng].append((fn, waits, list(incs), tag))

    def val(self, s):
        return self.cnt[s]


def build(nc):
    p = Plan()

    x32 = nc.dram_tensor("x", [TPC, H], F32, kind="ExternalInput").ap()
    wg32 = nc.dram_tensor("wg", [128, NCI, HT, CW], F32, kind="ExternalInput").ap()
    wu32 = nc.dram_tensor("wu", [128, NCI, HT, CW], F32, kind="ExternalInput").ap()
    wb32 = nc.dram_tensor("wb", [128, NCO, HT, CW], F32, kind="ExternalInput").ap()
    wd32 = nc.dram_tensor("wd", [128, NDH, IT, DHC], F32, kind="ExternalInput").ap()
    ag32 = nc.dram_tensor("ag", [128, HT, 36], F32, kind="ExternalInput").ap()
    bp32 = nc.dram_tensor("bp", [32, H], F32, kind="ExternalInput").ap()
    br32 = nc.dram_tensor("br", [1, 36], F32, kind="ExternalInput").ap()
    id_h = nc.dram_tensor("iden", [128, 128], F16, kind="ExternalInput").ap()
    on_h = nc.dram_tensor("ones1", [1, 128], F16, kind="ExternalInput").ap()
    y32 = nc.dram_tensor("y", [TPC, H], F32, kind="ExternalOutput").ap()

    sg16 = nc.dram_tensor("sg16", [128, NCI, HT, CW], F16).ap()
    su16_ = nc.dram_tensor("su16", [128, NCI, HT, CW], F16).ap()
    sb16 = nc.dram_tensor("sb16", [128, NCO, HT, CW], F16).ap()
    sd16 = nc.dram_tensor("sd16", [128, NDH, IT, DHC], F16).ap()
    sgact = nc.dram_tensor("sgact", [NG, HT, 128, 4 * 128], F16).ap()
    thb = nc.dram_tensor("thb", [NG, 1, T * 128], F16).ap()

    ctx = ExitStack()
    e = ctx.enter_context

    s_all = e(nc.sbuf_tensor("s_all", [128, T * I], F16))
    xnT = e(nc.sbuf_tensor("xnT", [128, HT * T * 128], F16))
    wq = [e(nc.sbuf_tensor(f"wq{i_}", [128, HT * CW], F16)) for i_ in range(2)]
    xs = [e(nc.sbuf_tensor(f"xs{i_}", [128, 512], F32)) for i_ in range(2)]
    xn16 = [e(nc.sbuf_tensor(f"xn16{i_}", [128, 512], F16)) for i_ in range(2)]
    xc = [e(nc.sbuf_tensor(f"xc{i_}", [128, 512], F32)) for i_ in range(2)]
    outst = [e(nc.sbuf_tensor(f"outst{i_}", [128, 512], F32)) for i_ in range(2)]
    gst = [e(nc.sbuf_tensor(f"gst{i_}", [128, 4 * 512], F16)) for i_ in range(2)]
    pst = [e(nc.sbuf_tensor(f"pst{i_}", [128, 4 * 512], F16)) for i_ in range(2)]
    gsp = [e(nc.sbuf_tensor(f"gsp{i_}", [128, 512], F16)) for i_ in range(2)]
    ind = e(nc.sbuf_tensor("ind", [128, CCH], F16))
    bn6 = e(nc.sbuf_tensor("bn6", [128, 48], F32))
    bn2 = e(nc.sbuf_tensor("bn2", [128, 2], F32))
    mean4 = e(nc.sbuf_tensor("mean4", [128, T], F32))
    rstd4 = e(nc.sbuf_tensor("rstd4", [128, T], F32))
    blo = e(nc.sbuf_tensor("blo", [128, T], F32))
    bhi = e(nc.sbuf_tensor("bhi", [128, T], F32))
    bmid = e(nc.sbuf_tensor("bmid", [128, T], F32))
    bneg = e(nc.sbuf_tensor("bneg", [128, T], F32))
    bpr = e(nc.sbuf_tensor("bpr", [128, T], mybir.dt.int32))
    cb = e(nc.sbuf_tensor("cb", [128, 24], F32))
    acs = e(nc.sbuf_tensor("acs", [128, 24], F32))
    c4 = e(nc.sbuf_tensor("c4", [128, 2 * T], F32))
    thr4 = e(nc.sbuf_tensor("thr4", [128, T], F16))
    thrT4 = e(nc.sbuf_tensor("thrT4", [4, 128], F16))
    throw = e(nc.sbuf_tensor("throw", [1, T * 128], F16))
    TH = e(nc.sbuf_tensor("TH", [128, T * 128], F16))
    ohb = e(nc.sbuf_tensor("ohb", [128, T * E], F32))
    t4m = e(nc.sbuf_tensor("t4m", [128, T], F32))
    zm16 = e(nc.sbuf_tensor("zm16", [128, T * 32], F16))
    zTg = e(nc.sbuf_tensor("zTg", [32, T * 128], F16))
    agb = e(nc.sbuf_tensor("agb", [128, HT * 36], F16))
    bpk = e(nc.sbuf_tensor("bpk", [32, H], F16))
    brow = e(nc.sbuf_tensor("brow", [1, 36], F16))
    iden = e(nc.sbuf_tensor("idn", [128, 128], F16))
    ones = e(nc.sbuf_tensor("on1", [1, 128], F16))
    epsv = e(nc.sbuf_tensor("epsv", [128, 1], F32))
    negh = e(nc.sbuf_tensor("negh", [128, 1], F32))
    dumv = e(nc.sbuf_tensor("dumv", [128, 1], F32))
    dums = e(nc.sbuf_tensor("dums", [128, 1], F32))

    pb = [e(nc.psum_tensor(f"pb{i_}", [128, 512], F32)) for i_ in range(8)]

    names = ("q_gp", "q_sc", "s_pe", "s_ac", "s_dv",
             "wA", "wB", "xA", "xB", "cA", "cB", "gA", "gB", "pA", "pB",
             "oA", "oB", "cS", "cG", "tA", "tB")
    for name in names:
        p.sems[name] = e(nc.semaphore(name))
        p.cnt[name] = 0
    GP, SC, PE, ACT, DV = "q_gp", "q_sc", "s_pe", "s_ac", "s_dv"

    s_t = [s_all[:, t * I:(t + 1) * I] for t in range(T)]
    s3 = s_all.rearrange("p (t i) -> p t i", t=T)
    TH3 = TH.rearrange("p (t i) -> p t i", t=T)

    ms = {}

    def vflush(tail, tag):
        p.add("vector", (lambda a: lambda v: v.tensor_copy(dumv[:, 0:1], a))(tail),
              incs=[(DV, 1)], tag=tag + ":fl")

    def sflush(tail, tag, parts=128):
        p.add("scalar", (lambda a, pp: lambda s: s.copy(dums[0:pp, 0:1], a))(
            tail, parts), incs=[(ACT, 1)], tag=tag + ":fl")

    p.add("sync", lambda g: g.dma_start(iden[:, :], id_h), incs=[("cS", 16)], tag="iden")
    p.add("sync", lambda g: g.dma_start(ones[:, :], on_h), incs=[("cS", 16)], tag="ones")
    p.add("gpsimd", lambda g: g.dma_start(
        agb.rearrange("p (a b) -> p a b", a=HT), ag32), incs=[("cG", 16)], tag="agb")
    p.add("gpsimd", lambda g: g.dma_start(bpk[:, :], bp32), incs=[("cG", 16)], tag="bpk")
    p.add("gpsimd", lambda g: g.dma_start(brow[0:1, 0:36], br32), incs=[("cG", 16)], tag="brow")
    CONST_S = ("cS", 32)
    CONST_G = ("cG", 48)
    p.add("vector", lambda v: v.memset(epsv[:, :], 1e-5), incs=[(DV, 1)], tag="epsv")
    p.add("vector", lambda v: v.memset(negh[:, :], -0.5), incs=[(DV, 1)], tag="negh")
    p.add("vector", lambda v: v.tensor_copy(dumv[:, 0:1], negh[:, 0:1]),
          incs=[(DV, 1)], tag="cst:fl")
    CONST_DV = p.val(DV)

    class WStream:
        def __init__(self):
            self.n = 0
            self.scrv = {}
            self.consumed = {}

        def load(self, g0, n_free, src32, src16, scr_out, extra_waits=(), tag=""):
            i = self.n
            self.n += 1
            slot = i % 2
            wsem = "wA" if slot == 0 else "wB"
            dst = wq[slot][:, 0:n_free]

            def flat(ap):
                return ap.rearrange("p a b -> p (a b)") if len(ap.shape) == 3 else ap

            queue = "sync"
            waits = list(extra_waits)
            if i >= 2:
                waits.extend(self.consumed[i - 2])
            if g0:
                p.add("gpsimd", (lambda d, s: lambda g: g.dma_start(d, s))(
                    dst, flat(src32)),
                    waits=waits, incs=[(wsem, 16)], tag=f"wl32:{tag}")
                lv = (wsem, p.val(wsem))
                p.add("scalar", (lambda d, s: lambda g: g.dma_start(d, s))(
                    flat(scr_out), dst),
                    waits=[lv], incs=[(SC, 16)], tag=f"wscr:{tag}", dma=True)
                self.scrv[i] = (SC, p.val(SC))
            else:
                p.add(queue, (lambda d, s: lambda g: g.dma_start(d, s))(
                    dst, flat(src16)),
                    waits=waits, incs=[(wsem, 16)], tag=f"wl16:{tag}")
                lv = (wsem, p.val(wsem))
            return slot, lv, i

        def done(self, i, sem_val):
            self.consumed[i] = [sem_val] + ([self.scrv[i]] if i in self.scrv else [])

    ws = WStream()

    scr_done = {}

    def xrows(tile):
        return x32[tile * 128:(tile + 1) * 128, :]

    def yrows(tile):
        return y32[tile * 128:(tile + 1) * 128, :]

    bank_free = [[] for _ in range(8)]

    def use_bank(b, waits):
        waits.extend(bank_free[b])
        bank_free[b] = []

    n_ost = 0

    def w_mid(g, t, act_flush, tag):
        p.add("vector", (lambda t_: lambda v: v.tensor_tensor(
            bmid[:, t_:t_ + 1], blo[:, t_:t_ + 1], bhi[:, t_:t_ + 1], ALU.add))(t),
            waits=[(DV, p.val(DV))], incs=[(DV, 1)], tag=f"mid0:{tag}")
        p.add("vector", (lambda t_: lambda v: v.tensor_scalar(
            bmid[:, t_:t_ + 1], bmid[:, t_:t_ + 1], 0.5, None,
            ALU.mult, ALU.bypass))(t),
            waits=[(DV, p.val(DV))], incs=[(DV, 1)], tag=f"mid1:{tag}")
        p.add("vector", (lambda t_: lambda v: v.tensor_scalar(
            bneg[:, t_:t_ + 1], bmid[:, t_:t_ + 1], -1.0, None,
            ALU.mult, ALU.bypass))(t),
            waits=[(DV, p.val(DV))], incs=[(DV, 1)], tag=f"neg:{tag}")
        if act_flush:
            p.add("vector", (lambda t_: lambda v: v.tensor_tensor(
                dumv[:, 0:1], bmid[:, t_:t_ + 1], bneg[:, t_:t_ + 1], ALU.add))(t),
                waits=[(DV, p.val(DV))], incs=[(DV, 1)], tag=f"midfl:{tag}")
        ms[f"mid:{tag}"] = p.val(DV)

    def w_upd(g, t, j, kconst, waits, last, phase):
        p.add("vector", (lambda t_: lambda v: v.tensor_tensor(
            c4[:, 2 * t_:2 * t_ + 1], c4[:, 2 * t_:2 * t_ + 1],
            c4[:, 2 * t_ + 1:2 * t_ + 2], ALU.subtract))(t),
            waits=waits, incs=[(DV, 1)], tag=f"d:{phase}:{g}:{t}:{j}")
        p.add("vector", (lambda t_, kk_: lambda v: v.tensor_scalar(
            bpr[:, t_:t_ + 1], c4[:, 2 * t_:2 * t_ + 1], kk_, None,
            ALU.is_lt, ALU.bypass))(t, kconst),
            waits=[(DV, p.val(DV))], incs=[(DV, 1)], tag=f"pred:{phase}:{g}:{t}:{j}")
        p.add("vector", (lambda t_: lambda v: v.select(
            blo[:, t_:t_ + 1], bpr[:, t_:t_ + 1], bmid[:, t_:t_ + 1],
            blo[:, t_:t_ + 1]))(t),
            waits=[(DV, p.val(DV))], incs=[(DV, 1)], tag=f"slo:{phase}:{g}:{t}:{j}")
        p.add("vector", (lambda t_: lambda v: v.select(
            bhi[:, t_:t_ + 1], bpr[:, t_:t_ + 1], bhi[:, t_:t_ + 1],
            bmid[:, t_:t_ + 1]))(t),
            waits=[(DV, p.val(DV))], incs=[(DV, 1)], tag=f"shi:{phase}:{g}:{t}:{j}")
        if phase == "F" and last:
            p.add("vector", (lambda t_: lambda v: v.tensor_tensor(
                bmid[:, t_:t_ + 1], blo[:, t_:t_ + 1], bhi[:, t_:t_ + 1], ALU.add))(t),
                waits=[(DV, p.val(DV))], incs=[(DV, 1)], tag=f"thm:{g}:{t}")
            p.add("vector", (lambda t_: lambda v: v.tensor_scalar(
                thr4[:, t_:t_ + 1], bmid[:, t_:t_ + 1], 0.5, None,
                ALU.mult, ALU.bypass))(t),
                waits=[(DV, p.val(DV))], incs=[(DV, 1)], tag=f"thr:{g}:{t}")
        elif not last:
            w_mid(g, t, phase == "F" and t in WACT, f"{phase}:{g}:{t}:{j + 1}")

    for g in range(NG):
        tiles = [g * T + t for t in range(T)]
        g0 = (g == 0)

        for t, tile in enumerate(tiles):
            for q in range(8):
                slot = q % 2
                w = []
                if q >= 2:
                    w.append((DV, ms[f"bn:{tile}:{q - 2}"]))
                elif f"p1done:{tile - 1}" in ms:
                    w.append((DV, ms[f"p1done:{tile - 1}"]))
                xsem = "xA" if slot == 0 else "xB"
                p.add("sync", (lambda sl, ap: lambda s: s.dma_start(xs[sl][:, :], ap))(
                    slot, xrows(tile)[:, q * 512:(q + 1) * 512]),
                    waits=w, incs=[(xsem, 16)], tag=f"xstat:{tile}:{q}")
                w2 = [(xsem, p.val(xsem))]
                if q == 0 and f"p1done:{tile - 1}" in ms:
                    w2.append((ACT, ms[f"rstd:{tile - 1}"]))
                p.add("vector", (lambda sl, q_: lambda v: v.bn_stats(
                    bn6[:, q_ * 6:(q_ + 1) * 6], xs[sl][:, :]))(slot, q),
                    waits=w2, incs=[(DV, 1)], tag=f"bn:{tile}:{q}")
                ms[f"bn:{tile}:{q}"] = p.val(DV)
            p.add("vector", lambda v: v.bn_aggr(bn2[:, :], bn6[:, :]),
                  waits=[(DV, ms[f"bn:{tile}:7"])], incs=[(DV, 1)], tag=f"aggr:{tile}")
            vflush(bn2[:, 1:2], f"aggr:{tile}")
            ms[f"aggr:{tile}"] = p.val(DV)
            p.add("vector", (lambda t_: lambda v: v.tensor_copy(
                mean4[:, t_:t_ + 1], bn2[:, 0:1]))(t),
                waits=[(DV, ms[f"aggr:{tile}"])], incs=[(DV, 1)], tag=f"mean:{tile}")
            ms[f"mean:{tile}"] = p.val(DV)
            p.add("scalar", (lambda t_: lambda s: s.activation(
                rstd4[:, t_:t_ + 1], bn2[:, 1:2], AF.Ln, bias=epsv[:, 0:1]))(t),
                waits=[(DV, ms[f"aggr:{tile}"]), (DV, CONST_DV)], incs=[(ACT, 1)],
                tag=f"lnv:{tile}")
            lnv = p.val(ACT)
            p.add("scalar", (lambda t_: lambda s: s.activation(
                rstd4[:, t_:t_ + 1], rstd4[:, t_:t_ + 1], AF.Exp, scale=negh[:, 0:1]))(t),
                waits=[(ACT, lnv)], incs=[(ACT, 1)], tag=f"rstd:{tile}")
            sflush(rstd4[:, t:t + 1], f"rstd:{tile}")
            ms[f"rstd:{tile}"] = p.val(ACT)

            for q in range(8):
                slot = q % 2
                w = [(DV, ms[f"nrm:{tile}:{q - 2}"])] if q >= 2 else \
                    [(DV, ms[f"bn:{tile}:{6 + slot}"])]
                xsem = "xA" if slot == 0 else "xB"
                p.add("sync", (lambda sl, ap: lambda s: s.dma_start(xs[sl][:, :], ap))(
                    slot, xrows(tile)[:, q * 512:(q + 1) * 512]),
                    waits=w, incs=[(xsem, 16)], tag=f"xn:{tile}:{q}")
                w2 = [(xsem, p.val(xsem)), (ACT, ms[f"rstd:{tile}"]), (DV, ms[f"mean:{tile}"])]
                if q >= 2:
                    w2.append((PE, ms[f"xtr:{tile}:{q - 2}"]))
                elif f"p1done:{tile - 1}" in ms:
                    w2.append((PE, ms[f"xtrl:{tile - 1}"]))
                p.add("vector", (lambda sl, t_: lambda v: v.tensor_scalar(
                    xn16[sl][:, :], xs[sl][:, :], mean4[:, t_:t_ + 1],
                    rstd4[:, t_:t_ + 1], ALU.subtract, ALU.mult))(slot, t),
                    waits=w2, incs=[(DV, 1)], tag=f"nrm:{tile}:{q}")
                vflush(xn16[slot][:, 511:512], f"nrm:{tile}:{q}")
                ms[f"nrm:{tile}:{q}"] = p.val(DV)
                bank = pb[q % 2]
                w3 = [(DV, ms[f"nrm:{tile}:{q}"]), CONST_S]
                if q >= 2:
                    w3.append((ACT, ms[f"xev:{tile}:{q - 2}"]))
                elif f"p1done:{tile - 1}" in ms:
                    w3.append((ACT, ms[f"xev:{tile - 1}:{6 + q}"]))
                use_bank(q % 2, w3)
                p.add("tensor", (lambda bk, sl: lambda tt: [tt.matmul(
                    bk[:, b_ * 128:(b_ + 1) * 128],
                    xn16[sl][:, b_ * 128:(b_ + 1) * 128],
                    iden[:, :], start=True, stop=True) for b_ in range(4)][-1])(
                    bank, slot),
                    waits=w3, incs=[(PE, 1)], tag=f"xtrb:{tile}:{q}")
                pev = p.val(PE)
                p.add("scalar", (lambda bk, q_, t_: lambda s: s.copy(
                    xnT.rearrange("p (ht x) -> p ht x", x=512)[
                        :, 4 * q_:4 * q_ + 4, t_ * 128:(t_ + 1) * 128],
                    bk[:, 0:512].rearrange("p (a b) -> p a b", a=4)))(bank, q, t),
                    waits=[(PE, pev)], incs=[(ACT, 1)], tag=f"xev:{tile}:{q}")
                if q == 7:
                    sflush(xnT[:, 31 * 512 + (t + 1) * 128 - 1:
                               31 * 512 + (t + 1) * 128], f"xev:{tile}")
                ms[f"xev:{tile}:{q}"] = p.val(ACT)
                ms[f"xtr:{tile}:{q}"] = p.val(PE)
            ms[f"xtrl:{tile}"] = p.val(PE)
            ms[f"p1done:{tile}"] = p.val(DV)
        for b_ in range(2):
            bank_free[b_] = [(ACT, p.val(ACT))]
        xnT_ready = [(ACT, ms[f"xev:{tile}:7"]) for tile in tiles]

        for c in range(NCI):
            ew = [] if g0 else [(SC, scr_done["g"])]
            slot, lv, li = ws.load(g0, HT * CW, wg32[:, c], sg16[:, c], sg16[:, c],
                                    extra_waits=ew, tag=f"g{g}c{c}")
            for t in range(T):
                k = c * T + t
                w = [lv]
                if k >= 8:
                    w.append((ACT, ms[f"sev:{g}:{k - 8}"]))
                if c == 0:
                    w.append(xnT_ready[t])
                use_bank(k % 8, w)
                p.add("tensor", (lambda bank, sl, t_: lambda tt: [
                    tt.matmul(bank[:, 0:CW],
                              xnT[:, ht * 512 + t_ * 128: ht * 512 + (t_ + 1) * 128],
                              wq[sl][:, ht * CW:(ht + 1) * CW],
                              start=(ht == 0), stop=(ht == HT - 1))
                    for ht in range(HT)][-1])(pb[k % 8], slot, t),
                    waits=w, incs=[(PE, 1)], tag=f"gmm:{g}:{k}")
                pev = p.val(PE)
                p.add("scalar", (lambda bank, t_, c_: lambda s: s.activation(
                    s_t[t_][:, c_ * CW:(c_ + 1) * CW], bank[:, 0:CW], AF.Silu))(
                    pb[k % 8], t, c),
                    waits=[(PE, pev)], incs=[(ACT, 1)], tag=f"sev:{g}:{k}")
                ms[f"sev:{g}:{k}"] = p.val(ACT)
            ws.done(li, (ACT, p.val(ACT)))
        if g0:
            scr_done["g"] = p.val(SC)
        sflush(s_t[T - 1][:, I - 1:I], f"gate:{g}")
        gate_all = (ACT, p.val(ACT))
        for b_ in range(8):
            bank_free[b_] = [gate_all]

        for t in range(T):
            p.add("vector", (lambda t_: lambda v: v.tensor_copy(
                ind[:, 0:512], s_t[t_][:, SUBW - 512:SUBW]))(t),
                waits=[(ACT, ms[f"sev:{g}:{7 * T + t}"])], incs=[(DV, 1)],
                tag=f"spad:{g}:{t}")
        for t in range(T):
            p.add("vector", (lambda t_: lambda v: v.memset(blo[:, t_:t_ + 1], LO0))(t),
                  waits=[(DV, p.val(DV))], incs=[(DV, 1)], tag=f"sblo:{g}:{t}")
            p.add("vector", (lambda t_: lambda v: v.memset(bhi[:, t_:t_ + 1], HI0))(t),
                  incs=[(DV, 1)], tag=f"sbhi:{g}:{t}")
            w_mid(g, t, False, f"S:{g}:{t}:0")
        for j in range(SUB_IT):
            for t in range(T):
                mv = ms[f"mid:S:{g}:{t}:{j}"]
                p.add("vector", (lambda t_: lambda v: v.tensor_scalar(
                    ind[:, 0:SUBW], s_t[t_][:, 0:SUBW], bmid[:, t_:t_ + 1], None,
                    ALU.is_le, ALU.add, accum_out=c4[:, 2 * t_:2 * t_ + 1]))(t),
                    waits=[(DV, mv)], incs=[(DV, 1)], tag=f"sc1:{g}:{t}:{j}")
                p.add("vector", (lambda t_: lambda v: v.tensor_scalar(
                    ind[:, 0:SUBW], s_t[t_][:, 0:SUBW], bneg[:, t_:t_ + 1], None,
                    ALU.is_lt, ALU.add, accum_out=c4[:, 2 * t_ + 1:2 * t_ + 2]))(t),
                    waits=[(DV, p.val(DV))], incs=[(DV, 1)], tag=f"sc2:{g}:{t}:{j}")
                w_upd(g, t, j, KSUB, [(DV, p.val(DV))], j == SUB_IT - 1, "S")
        for t in range(T):
            p.add("vector", (lambda t_: lambda v: v.tensor_scalar(
                blo[:, t_:t_ + 1], blo[:, t_:t_ + 1], WIDEN, 0.0,
                ALU.subtract, ALU.max))(t),
                waits=[(DV, p.val(DV))], incs=[(DV, 1)], tag=f"wlo:{g}:{t}")
            p.add("vector", (lambda t_: lambda v: v.tensor_scalar(
                bhi[:, t_:t_ + 1], bhi[:, t_:t_ + 1], WIDEN, None,
                ALU.add, ALU.bypass))(t),
                waits=[(DV, p.val(DV))], incs=[(DV, 1)], tag=f"whi:{g}:{t}")
            w_mid(g, t, t in WACT, f"F:{g}:{t}:0")
        for t in range(T):
            p.add("vector", (lambda t_: lambda v: v.tensor_copy(
                ind[:, 0:512], s_t[t_][:, I - 512:I]))(t),
                waits=[gate_all], incs=[(DV, 1)], tag=f"fpad:{g}:{t}")
        pads_done = p.val(DV)

        def mk_dve_unit(t, j):
            def emit():
                mv = ms[f"mid:F:{g}:{t}:{j}"]
                for q, (off, w_) in enumerate(CCHUNKS):
                    p.add("vector", (lambda t_, o_, w__, q_: lambda v: v.tensor_scalar(
                        ind[:, 0:w__], s_t[t_][:, o_:o_ + w__], bmid[:, t_:t_ + 1],
                        None, ALU.is_le, ALU.add,
                        accum_out=cb[:, t_ * 12 + q_:t_ * 12 + q_ + 1]))(t, off, w_, q),
                        waits=[(DV, mv), (DV, pads_done)], incs=[(DV, 1)],
                        tag=f"fc1:{g}:{t}:{j}:{q}")
                for q, (off, w_) in enumerate(CCHUNKS):
                    p.add("vector", (lambda t_, o_, w__, q_: lambda v: v.tensor_scalar(
                        ind[:, 0:w__], s_t[t_][:, o_:o_ + w__], bneg[:, t_:t_ + 1],
                        None, ALU.is_lt, ALU.add,
                        accum_out=cb[:, t_ * 12 + 6 + q_:t_ * 12 + 7 + q_]))(t, off, w_, q),
                        waits=[(DV, p.val(DV))], incs=[(DV, 1)],
                        tag=f"fc2:{g}:{t}:{j}:{q}")
                allc = p.val(DV)
                p.add("vector", (lambda t_: lambda v: v.reduce_sum(
                    c4[:, 2 * t_:2 * t_ + 1], cb[:, t_ * 12:t_ * 12 + 6], AX))(t),
                    waits=[(DV, allc)], incs=[(DV, 1)], tag=f"fr1:{g}:{t}:{j}")
                p.add("vector", (lambda t_: lambda v: v.reduce_sum(
                    c4[:, 2 * t_ + 1:2 * t_ + 2], cb[:, t_ * 12 + 6:t_ * 12 + 12],
                    AX))(t),
                    waits=[(DV, p.val(DV))], incs=[(DV, 1)], tag=f"fr2:{g}:{t}:{j}")
                ms[f"fcnt:{g}:{t}:{j}"] = p.val(DV)
                w_upd(g, t, j, KFULL, [(DV, p.val(DV))], j == FULL_IT - 1, "F")
            return emit

        def mk_act_chunks(t, j, qr):
            def emit():
                mv = ms[f"mid:F:{g}:{t}:{j}"]
                a0 = (t - 2) * 12
                for qq in qr:
                    side, q = qq // 6, qq % 6
                    off, w_ = CCHUNKS[q]
                    bias = bneg if side == 0 else bmid
                    p.add("scalar", (lambda t_, o_, w__, b_, sl_: lambda s: s.activation(
                        pst[0][:, 0:w__], s_t[t_][:, o_:o_ + w__], AF.Sign,
                        bias=b_[:, t_:t_ + 1],
                        accum_out=acs[:, sl_:sl_ + 1]))(t, off, w_, bias, a0 + qq),
                        waits=[(DV, mv)], incs=[(ACT, 1)], tag=f"ac:{g}:{t}:{j}:{qq}")
            return emit

        def mk_act_tail(t, j):
            def emit():
                a0 = (t - 2) * 12
                sflush(acs[:, a0 + 11:a0 + 12], f"acfl:{g}:{t}:{j}")
                fv = p.val(ACT)
                ms[f"acnt:{g}:{t}:{j}"] = fv
                p.add("vector", (lambda t_, a_: lambda v: v.reduce_sum(
                    c4[:, 2 * t_:2 * t_ + 1], acs[:, a_ + 6:a_ + 12], AX))(t, a0),
                    waits=[(ACT, fv), (DV, p.val(DV))], incs=[(DV, 1)],
                    tag=f"ar2:{g}:{t}:{j}")
                p.add("vector", (lambda t_, a_: lambda v: v.reduce_sum(
                    c4[:, 2 * t_ + 1:2 * t_ + 2], acs[:, a_:a_ + 6], AX))(t, a0),
                    waits=[(DV, p.val(DV))], incs=[(DV, 1)], tag=f"ar1:{g}:{t}:{j}")
                w_upd(g, t, j, KACT, [(DV, p.val(DV))], j == FULL_IT - 1, "F")
            return emit

        master = []
        for j in range(FULL_IT):
            for td, ta in ((0, 2), (1, 3)):
                master.append(mk_dve_unit(td, j))
                master.append(mk_act_chunks(ta, j, range(0, 4)))
                master.append(mk_act_chunks(ta, j, range(4, 8)))
                master.append(mk_act_chunks(ta, j, range(8, 12)))
                master.append(mk_act_tail(ta, j))
        m_emitted = 0

        for t, tile in enumerate(tiles):
            bank = pb[t]
            w = [gate_all, CONST_G, CONST_S]
            use_bank(t, w)
            p.add("tensor", (lambda bk, t_: lambda tt: (
                [tt.matmul(bk[:, 0:36],
                           xnT[:, ht * 512 + t_ * 128: ht * 512 + (t_ + 1) * 128],
                           agb[:, ht * 36:(ht + 1) * 36],
                           start=(ht == 0), stop=False) for ht in range(HT)],
                tt.matmul(bk[:, 0:36], ones[0:1, :], brow[0:1, 0:36],
                          start=False, stop=True))[1])(bank, t),
                waits=w, incs=[(PE, 1)], tag=f"zl:{tile}")
            pev = p.val(PE)
            p.add("vector", (lambda bk, t_: lambda v: v.reduce_max(
                t4m[:, t_:t_ + 1], bk[:, 32:36], AX))(bank, t),
                waits=[(PE, pev)], incs=[(DV, 1)], tag=f"rmax:{tile}")
            rv = p.val(DV)
            p.add("vector", (lambda bk, t_: lambda v: v.tensor_scalar(
                ohb[:, t_ * E:(t_ + 1) * E], bk[:, 32:36], t4m[:, t_:t_ + 1], None,
                ALU.is_ge, ALU.bypass))(bank, t),
                waits=[(DV, rv)], incs=[(DV, 1)], tag=f"oh:{tile}")
            ov = p.val(DV)
            for ee in range(E):
                p.add("vector", (lambda bk, t_, e_: lambda v: v.tensor_scalar(
                    zm16[:, t_ * 32 + e_ * R: t_ * 32 + (e_ + 1) * R],
                    bk[:, e_ * R:(e_ + 1) * R],
                    ohb[:, t_ * E + e_: t_ * E + e_ + 1], None,
                    ALU.mult, ALU.bypass))(bank, t, ee),
                    waits=[(DV, ov)], incs=[(DV, 1)], tag=f"zm:{tile}:{ee}")
            vflush(zm16[:, t * 32 + 31: t * 32 + 32], f"zm:{tile}")
            zmv = p.val(DV)
            bank2 = pb[4 + (t % 2)]
            w = [(DV, zmv)]
            if t >= 2:
                w.append((ACT, ms[f"zev:{tiles[t - 2]}"]))
            use_bank(4 + (t % 2), w)
            p.add("tensor", (lambda b2, t_: lambda tt: tt.matmul(
                b2[0:32, 0:128], zm16[:, t_ * 32:(t_ + 1) * 32], iden[:, :],
                start=True, stop=True))(bank2, t),
                waits=w, incs=[(PE, 1)], tag=f"ztr:{tile}")
            pez = p.val(PE)
            p.add("scalar", (lambda b2, t_: lambda s: s.copy(
                zTg[0:32, t_ * 128:(t_ + 1) * 128], b2[0:32, 0:128]))(bank2, t),
                waits=[(PE, pez)], incs=[(ACT, 1)], tag=f"zev:{tile}")
            if t == T - 1:
                sflush(zTg[0:32, T * 128 - 1:T * 128], f"zev:{tile}", parts=32)
            ms[f"zev:{tile}"] = p.val(ACT)
        zT_ready = (ACT, p.val(ACT))
        zread_done = (DV, p.val(DV))
        for b_ in range(4):
            bank_free[b_] = [zread_done]
        for b_ in (4, 5):
            bank_free[b_] = [zT_ready]

        for c in range(NCO):
            ew = [] if g0 else [(SC, scr_done["b"])]
            slot, lv, li = ws.load(g0, HT * CW, wb32[:, c], sb16[:, c], sb16[:, c],
                                    extra_waits=ew, tag=f"b{g}c{c}")
            for ob in range(2):
                og = c * 2 + ob
                bk = pb[4 + og % 2]
                w = [lv, zT_ready]
                if og >= 2:
                    w.append((ACT, ms[f"gev:{g}:{og - 2}"]))
                use_bank(4 + og % 2, w)
                p.add("tensor", (lambda bk_, sl, ob_, og_: lambda tt: (
                    [tt.matmul(bk_[:, :],
                               wq[sl][:, ht * CW + ob_ * 128: ht * CW + (ob_ + 1) * 128],
                               xnT[:, ht * 512:(ht + 1) * 512],
                               start=(ht == 0), stop=False) for ht in range(HT)],
                    tt.matmul(bk_[:, :], bpk[0:32, og_ * 128:(og_ + 1) * 128],
                              zTg[0:32, :], start=False, stop=True))[1])(bk, slot, ob, og),
                    waits=w, incs=[(PE, 1)], tag=f"bmm:{g}:{og}")
                pev = p.val(PE)
                gslot = og % 2
                w2 = [(PE, pev)]
                if og >= 2:
                    w2.append(ms[f"gsp:{g}:{og - 2}"])
                elif g > 0:
                    w2.append(ms[f"gsp:{g - 1}:{2 * NCO - 2 + og}"])
                p.add("scalar", (lambda bk_, gs: lambda s: s.activation(
                    gsp[gs][:, :], bk_[:, :], AF.Silu))(bk, gslot),
                    waits=w2, incs=[(ACT, 1)], tag=f"gev:{g}:{og}")
                sflush(gsp[gslot][:, 511:512], f"gev:{g}:{og}")
                ms[f"gev:{g}:{og}"] = p.val(ACT)
                av = p.val(ACT)
                psem = "pA" if gslot == 0 else "pB"
                p.add("scalar", (lambda gs, g_, og_: lambda s: s.dma_start(
                    sgact[g_, og_], gsp[gs][:, :]))(gslot, g, og),
                    waits=[(ACT, av)], incs=[(psem, 16)], tag=f"gsp:{g}:{og}", dma=True)
                ms[f"gsp:{g}:{og}"] = (psem, p.val(psem))
                tgt = (og + 1) * len(master) // (2 * NCO)
                while m_emitted < tgt:
                    master[m_emitted]()
                    m_emitted += 1
            ws.done(li, (ACT, p.val(ACT)))
        if g0:
            scr_done["b"] = p.val(SC)
        while m_emitted < len(master):
            master[m_emitted]()
            m_emitted += 1
        gact_pA = ("pA", p.val("pA"))
        gact_pB = ("pB", p.val("pB"))
        bank_free[4] = [(ACT, ms[f"gev:{g}:{2 * NCO - 2}"])]
        bank_free[5] = [(ACT, ms[f"gev:{g}:{2 * NCO - 1}"])]
        cnt_done_dv = (DV, p.val(DV))

        p.add("vector", lambda v: v.reduce_sum(dumv[:, 0:1], thr4[:, 0:T], AX),
              waits=[(DV, p.val(DV))], incs=[(DV, 1)], tag=f"th4fl:{g}")
        th4fl = p.val(DV)
        w = [(DV, th4fl), CONST_S]
        use_bank(6, w)
        p.add("tensor", lambda tt: tt.matmul(
            pb[6][0:4, 0:128], thr4[:, 0:T], iden[:, :], start=True, stop=True),
            waits=w, incs=[(PE, 1)], tag=f"thtr:{g}")
        pev = p.val(PE)
        p.add("scalar", lambda s: s.copy(thrT4[0:4, :], pb[6][0:4, 0:128]),
              waits=[(PE, pev)], incs=[(ACT, 1)], tag=f"thev:{g}")
        sflush(thrT4[0:4, 127:128], f"thev:{g}", parts=4)
        thev = p.val(ACT)
        w = [(ACT, thev)]
        if g > 0:
            w.append(("tB", 16 * g))
        p.add("sync", (lambda g_: lambda s: s.dma_start(
            thb[g_, 0].rearrange("(a b) -> a b", a=4), thrT4[0:4, :]))(g),
            waits=w, incs=[("tA", 16)], tag=f"thdo:{g}")
        w = [("tA", p.val("tA"))]
        if g > 0:
            w.append((PE, ms[f"thbc:{g - 1}"]))
        p.add("sync", (lambda g_: lambda s: s.dma_start(throw[0:1, :], thb[g_]))(g),
            waits=w, incs=[("tB", 16)], tag=f"thdi:{g}")
        w = [("tB", p.val("tB")), CONST_S]
        use_bank(6, w)
        p.add("tensor", lambda tt: tt.matmul(
            pb[6][:, 0:T * 128], ones[0:1, :], throw[0:1, :], start=True, stop=True),
            waits=w, incs=[(PE, 1)], tag=f"thbc:{g}")
        ms[f"thbc:{g}"] = p.val(PE)
        w = [(PE, ms[f"thbc:{g}"])]
        if g > 0:
            w.append((DV, ms[f"m:{g - 1}:{2 * NCI - 1}"]))
        p.add("scalar", lambda s: s.copy(TH[:, :], pb[6][:, 0:T * 128]),
              waits=w, incs=[(ACT, 1)], tag=f"THev:{g}")
        sflush(TH[:, T * 128 - 1:T * 128], f"THev:{g}")
        th_ready = p.val(ACT)
        bank_free[6] = [(ACT, th_ready)]

        for c in range(NCI):
            ew = [] if g0 else [(SC, scr_done["u"])]
            slot, lv, li = ws.load(g0, HT * CW, wu32[:, c], su16_[:, c], su16_[:, c],
                                    extra_waits=ew, tag=f"u{g}c{c}")
            for bb in range(2):
                it = c * 2 + bb
                ub = pb[it % 3]
                tb = pb[3 + it % 2]
                w = [gate_all]
                if it >= 2:
                    w.append((ACT, ms[f"smev:{g}:{it - 2}"]))
                use_bank(3 + it % 2, w)
                p.add("tensor", (lambda it_, tb_: lambda tt: [
                    tt.matmul(tb_[:, t_ * 128:(t_ + 1) * 128],
                              s3[:, t_, it_ * 128:(it_ + 1) * 128], iden[:, :],
                              start=True, stop=True) for t_ in range(T)][-1])(it, tb),
                    waits=w, incs=[(PE, 1)], tag=f"smtr:{g}:{it}")
                ptr = p.val(PE)
                p.add("scalar", (lambda tb_, it_: lambda s: s.copy(
                    s3[:, :, it_ * 128:(it_ + 1) * 128],
                    tb_[:, :].rearrange("p (t i) -> p t i", t=T)))(tb, it),
                    waits=[(PE, ptr), cnt_done_dv], incs=[(ACT, 1)],
                    tag=f"smev:{g}:{it}")
                ms[f"smev:{g}:{it}"] = p.val(ACT)
                ar = (it % 2) * 512
                mr = 1024 + (it % 2) * 512
                w2 = [(ACT, ms[f"smev:{g}:{it}"])]
                if it >= 2:
                    w2.append((DV, ms[f"m:{g}:{it - 2}"]))
                elif g > 0:
                    w2.append((DV, ms[f"m:{g - 1}:{2 * NCI - 2 + it}"]))
                p.add("scalar", (lambda it_, ar_: lambda s: s.activation(
                    ind[:, ar_:ar_ + 512].rearrange("p (t i) -> p t i", t=T),
                    s3[:, :, it_ * 128:(it_ + 1) * 128], AF.Abs))(it, ar),
                    waits=w2, incs=[(ACT, 1)], tag=f"sab:{g}:{it}")
                sflush(ind[:, ar + 511:ar + 512], f"sab:{g}:{it}")
                abfl = p.val(ACT)
                p.add("vector", (lambda ar_, mr_: lambda v: v.tensor_tensor(
                    ind[:, mr_:mr_ + 512].rearrange("p (t i) -> p t i", t=T),
                    ind[:, ar_:ar_ + 512].rearrange("p (t i) -> p t i", t=T),
                    TH3[:, :, :],
                    ALU.is_gt))(ar, mr),
                    waits=[(ACT, abfl), (ACT, th_ready), (DV, p.val(DV))],
                    incs=[(DV, 1)], tag=f"m:{g}:{it}")
                ms[f"m:{g}:{it}"] = p.val(DV)
                p.add("vector", (lambda it_, mr_: lambda v: v.tensor_tensor(
                    s3[:, :, it_ * 128:(it_ + 1) * 128],
                    ind[:, mr_:mr_ + 512].rearrange("p (t i) -> p t i", t=T),
                    s3[:, :, it_ * 128:(it_ + 1) * 128], ALU.mult))(it, mr),
                    waits=[(DV, p.val(DV))], incs=[(DV, 1)], tag=f"sm:{g}:{it}")
                smv = p.val(DV)
                w = [lv]
                if it >= 3:
                    w.append((DV, ms[f"prod:{g}:{it - 3}"]))
                use_bank(it % 3, w)
                p.add("tensor", (lambda sl, bb_, ub_: lambda tt: [
                    tt.matmul(ub_[:, :],
                              wq[sl][:, ht * CW + bb_ * 128: ht * CW + (bb_ + 1) * 128],
                              xnT[:, ht * 512:(ht + 1) * 512],
                              start=(ht == 0), stop=(ht == HT - 1))
                    for ht in range(HT)][-1])(slot, bb, ub),
                    waits=w, incs=[(PE, 1)], tag=f"ump:{g}:{it}")
                pup = p.val(PE)
                p.add("vector", (lambda it_, ub_: lambda v: v.tensor_tensor(
                    s3[:, :, it_ * 128:(it_ + 1) * 128],
                    s3[:, :, it_ * 128:(it_ + 1) * 128],
                    ub_[:, :].rearrange("p (t i) -> p t i", t=T), ALU.mult))(it, ub),
                    waits=[(PE, pup), (DV, smv)], incs=[(DV, 1)],
                    tag=f"prod:{g}:{it}")
                vflush(s_all[:, (T - 1) * I + it * 128 + 127:
                             (T - 1) * I + it * 128 + 128], f"prod:{g}:{it}")
                ms[f"prod:{g}:{it}"] = p.val(DV)
            ws.done(li, (PE, p.val(PE)))
        if g0:
            scr_done["u"] = p.val(SC)
        prod_all = (DV, p.val(DV))
        for b_ in range(8):
            bank_free[b_] = [prod_all]

        pend_epi = []
        for hc in range(NDH):
            bset_i = [4 * (hc % 2) + i_ for i_ in range(4)]
            bset = [pb[i_] for i_ in bset_i]
            gslot = hc % 2
            w = [gact_pA, gact_pB]
            if hc >= 2:
                w.append((DV, ms[f"pprod:{g}:{hc - 2}"]))
            elif g > 0:
                w.append((DV, ms[f"pprod:{g - 1}:{NDH - 2 + hc}"]))
            gsem = "gA" if gslot == 0 else "gB"
            p.add("scalar", (lambda g_, hc_, gs: lambda s: s.dma_start(
                gst[gs][:, :].rearrange("p (a b) -> p a b", a=4),
                sgact[g_].rearrange("a p b -> p a b")[:, 4 * hc_:4 * hc_ + 4, :]))(
                g, hc, gslot),
                waits=w, incs=[(gsem, 16)], tag=f"gfill:{g}:{hc}", dma=True)
            gfv = (gsem, p.val(gsem))
            for gi, (it0, itn) in enumerate(ITGS):
                ew = [] if g0 else [(SC, scr_done["d"])]
                slot, lv, li = ws.load(g0, itn * DHC, wd32[:, hc, it0:it0 + itn, :],
                                        sd16[:, hc, it0:it0 + itn, :],
                                        sd16[:, hc, it0:it0 + itn, :],
                                        extra_waits=ew, tag=f"d{g}h{hc}i{gi}")
                w = [lv, prod_all]
                if gi == 0:
                    for b_ in bset_i:
                        use_bank(b_, w)
                    if hc >= 2:
                        w.append((ACT, ms[f"sav:{g}:{hc - 2}"]))
                    if hc >= 3:
                        w.append((DV, ms[f"cmb:{g}:{hc - 3}:{T - 1}"]))
                p.add("tensor", (lambda sl, it0_, itn_, bs, first: lambda tt: [
                    tt.matmul(bs[hb][:, :],
                              wq[sl][:, il * DHC + hb * 128: il * DHC + (hb + 1) * 128],
                              s3[:, :, (it0_ + il) * 128:(it0_ + il + 1) * 128],
                              start=(first and il == 0), stop=((it0_ + il) == IT - 1))
                    for il in range(itn_) for hb in range(4)][-1])(
                    slot, it0, itn, bset, gi == 0),
                    waits=w, incs=[(PE, 1)], tag=f"dmm:{g}:{hc}:{gi}")
                ws.done(li, (PE, p.val(PE)))
            pdn = p.val(PE)
            if len(pend_epi) == 2:
                pend_epi.pop(0)()
            pslot = hc % 2
            for hb in range(4):
                w = [(PE, p.val(PE) if hb == 0 else pdn)]
                if hb == 0 and hc >= 2:
                    w.append((DV, ms[f"pprod:{g}:{hc - 2}"]))
                elif hb == 0 and g > 0:
                    w.append((DV, ms[f"pprod:{g - 1}:{NDH - 2 + hc}"]))
                p.add("scalar", (lambda b_, ps, hb_: lambda s: s.copy(
                    pst[ps][:, hb_ * 512:(hb_ + 1) * 512], b_[:, :]))(
                    bset[hb], pslot, hb),
                    waits=w, incs=[(ACT, 1)], tag=f"sav:{g}:{hc}:{hb}")
            sev = p.val(ACT)
            ms[f"sav:{g}:{hc}"] = sev
            p.add("vector", (lambda ps: lambda v: v.tensor_tensor(
                pst[ps][:, :], pst[ps][:, :], gst[ps][:, :], ALU.mult))(pslot),
                waits=[(ACT, sev), gfv], incs=[(DV, 1)], tag=f"pprod:{g}:{hc}")
            vflush(pst[pslot][:, 2047:2048], f"pprod:{g}:{hc}")
            ms[f"pprod:{g}:{hc}"] = p.val(DV)
            ppv = p.val(DV)

            def mk_epi(hc, pslot, ppv):
                def emit():
                    nonlocal n_ost
                    for t, tile in enumerate(tiles):
                        tb = pb[4 * ((hc + 1) % 2) + t]
                        w0 = [(DV, ppv)]
                        if t == 0:
                            if f"sav:{g}:{hc + 1}" in ms:
                                w0.append((ACT, ms[f"sav:{g}:{hc + 1}"]))
                            if f"cmb:{g}:{hc - 2}:{T - 1}" in ms:
                                w0.append((DV, ms[f"cmb:{g}:{hc - 2}:{T - 1}"]))
                        p.add("tensor", (lambda ps, t_, tb_: lambda tt: [
                            tt.matmul(tb_[:, hb * 128:(hb + 1) * 128],
                                      pst[ps][:, hb * 512 + t_ * 128:
                                              hb * 512 + (t_ + 1) * 128],
                                      iden[:, :], start=True, stop=True)
                            for hb in range(4)][-1])(pslot, t, tb),
                            waits=w0, incs=[(PE, 1)], tag=f"ptr:{g}:{hc}:{t}")
                        ptv = p.val(PE)
                        idx = hc * T + t
                        xslot = idx % 2
                        w = []
                        if idx >= 2:
                            pi = idx - 2
                            w.append((DV, ms[f"cmb:{g}:{pi // T}:{pi % T}"]))
                        elif g > 0:
                            w.append((DV, ms[f"cmb:{g - 1}:{NDH - 1}:{(idx - 2) % T}"]))
                        csem = "cA" if xslot == 0 else "cB"
                        xyq = "gpsimd" if g > 0 else "sync"
                        p.add(xyq, (lambda xs_, tile_, hc_: lambda s: s.dma_start(
                            xc[xs_][:, :], xrows(tile_)[:, hc_ * 512:(hc_ + 1) * 512]))(
                            xslot, tile, hc),
                            waits=w, incs=[(csem, 16)], tag=f"xc:{g}:{hc}:{t}")
                        xlv = (csem, p.val(csem))
                        oslot = idx % 2
                        w = [(PE, ptv), xlv]
                        if n_ost >= 2:
                            osm = "oA" if n_ost % 2 == 0 else "oB"
                            w.append((osm, 16 * (n_ost // 2)))
                        p.add("vector", (lambda xs_, tb_, os_: lambda v: v.tensor_tensor(
                            outst[os_][:, :], xc[xs_][:, :], tb_[:, :], ALU.add))(
                            xslot, tb, oslot),
                            waits=w, incs=[(DV, 1)], tag=f"cmb:{g}:{hc}:{t}")
                        ms[f"cmb:{g}:{hc}:{t}"] = p.val(DV)
                        cv = p.val(DV)
                        osem = "oA" if oslot == 0 else "oB"
                        p.add(xyq, (lambda os_, tile_, hc_: lambda s: s.dma_start(
                            yrows(tile_)[:, hc_ * 512:(hc_ + 1) * 512], outst[os_][:, :]))(
                            oslot, tile, hc),
                            waits=[(DV, cv)], incs=[(osem, 16)], tag=f"ost:{g}:{hc}:{t}")
                        n_ost += 1
                return emit

            pend_epi.append(mk_epi(hc, pslot, ppv))
        for f_ in pend_epi:
            f_()
        pend_epi = []
        if g0:
            scr_done["d"] = p.val(SC)
        last_cmb = (DV, p.val(DV))
        for b_ in range(8):
            bank_free[b_] = [last_cmb]

    p.add("sync", lambda s: s.nop(),
          waits=[("oA", 16 * ((n_ost + 1) // 2)), ("oB", 16 * (n_ost // 2))],
          tag="final")

    with nc.Block() as block:
        def runner(name):
            def _run(eng):
                for fn, waits, incs, tag in p.ops[name]:
                    for s, v in waits:
                        eng.wait_ge(p.sems[s], v)
                    inst = fn(eng)
                    for s, k in incs:
                        inst.then_inc(p.sems[s], k)
            return _run
        block.tensor(runner("tensor"))
        block.vector(runner("vector"))
        block.scalar(runner("scalar"))
        block.gpsimd(runner("gpsimd"))
        block.sync(runner("sync"))

    ctx.close()
    return nc


def _blocked_hi(wT, nch):
    return np.ascontiguousarray(wT.reshape(HT, 128, nch, CW).transpose(1, 2, 0, 3))


def _blocked_down(wdT):
    return np.ascontiguousarray(wdT.reshape(IT, 128, NDH, DHC).transpose(1, 2, 0, 3))


def make_inputs(L0_attn_x, L0_attn_w_gate, L0_attn_w_up, L0_attn_w_down,
                L0_ffn_W_up, L0_ffn_gate_w, L0_ffn_expert_bias,
                L0_ffn_lora_A, L0_ffn_lora_B):
    x2 = np.ascontiguousarray(
        np.asarray(L0_attn_x, dtype=np.float32).reshape(8 * TPC, H))
    wg = _blocked_hi(np.ascontiguousarray(np.asarray(L0_attn_w_gate, np.float32).T), NCI)
    wu = _blocked_hi(np.ascontiguousarray(np.asarray(L0_attn_w_up, np.float32).T), NCI)
    wb = _blocked_hi(np.ascontiguousarray(np.asarray(L0_ffn_W_up, np.float32).T), NCO)
    wd = _blocked_down(np.ascontiguousarray(np.asarray(L0_attn_w_down, np.float32).T))
    A_all = np.asarray(L0_ffn_lora_A, np.float32).reshape(32, H)
    ag = np.concatenate([A_all.T,
                         np.ascontiguousarray(np.asarray(L0_ffn_gate_w, np.float32).T)],
                        axis=1)
    ag = np.ascontiguousarray(ag.reshape(HT, 128, 36).transpose(1, 0, 2))
    bp = np.ascontiguousarray(
        np.asarray(L0_ffn_lora_B, np.float32).transpose(0, 2, 1).reshape(32, H))
    br = np.concatenate([np.zeros((1, 32), np.float32),
                         np.asarray(L0_ffn_expert_bias, np.float32).reshape(1, 4)],
                        axis=1)
    shared = dict(wg=wg, wu=wu, wb=wb, wd=wd, ag=ag, bp=bp, br=br,
                  iden=np.eye(128, dtype=np.float16),
                  ones1=np.ones((1, 128), dtype=np.float16))
    return x2, shared


def kernel(**inputs):
    x2, shared = make_inputs(**inputs)
    nc = bass.Bass("TRN2", target_bir_lowering=False)
    build(nc)
    in_maps = []
    for c in range(8):
        m = dict(shared)
        m["x"] = np.ascontiguousarray(x2[c * TPC:(c + 1) * TPC])
        in_maps.append(m)
    res = run_bass_kernel_spmd(nc, in_maps, core_ids=list(range(8)))
    out = np.concatenate([r["y"] for r in res.results], axis=0)
    return out.reshape(4, 4096, 4096)
